# version 35
# baseline (speedup 1.0000x reference)
import numpy as np
import concourse.bass as bass
import concourse.mybir as mybir
from concourse.bass_utils import run_bass_kernel_spmd

F32 = mybir.dt.float32
BF16 = mybir.dt.bfloat16
FP8 = mybir.dt.float8e4
I32 = mybir.dt.int32
U8 = mybir.dt.uint8
AF = mybir.ActivationFunctionType
ALU = mybir.AluOpType
AXX = mybir.AxisListType.X

SHIFT = 12.0
FP8_MAX = 448.0
HALF_SMALL = 2.0 ** -5


def build_nc(S=4096, SQ=2048, D=1024, QB=256, dbg=False):
    NT = S // 128
    TPW = 4
    NW = NT // TPW
    NQB = SQ // QB
    NSUB = QB // 128
    KC = S // 128
    OC = D // 128
    OQ = D // 256
    inv_d = 1.0 / D
    NCONST = 7 * 16
    DLR = 16 * 8 if dbg else 0
    NPP1 = NW * 16
    NTR1 = NT

    nc = bass.Bass("TRN2", target_bir_lowering=False, detect_race_conditions=False)
    xb_h = nc.dram_tensor("xb", [S, D], F32, kind="ExternalInput").ap()
    xo_h = nc.dram_tensor("xo", [SQ, D], F32, kind="ExternalInput").ap()
    wqt_h = nc.dram_tensor("wqt", [128, OC, OC, 128], F32, kind="ExternalInput").ap()
    wkt_h = nc.dram_tensor("wkt", [128, OC, OC, 128], F32, kind="ExternalInput").ap()
    wvt_h = nc.dram_tensor("wvt", [128, OC, D], F32, kind="ExternalInput").ap()
    wot_h = nc.dram_tensor("wot", [128, OQ, OC, 256], F32, kind="ExternalInput").ap()
    wft_h = nc.dram_tensor("wft", [128, OQ, OC, 256], F32, kind="ExternalInput").ap()
    bq_h = nc.dram_tensor("bq", [D], F32, kind="ExternalInput").ap()
    bk_h = nc.dram_tensor("bk", [D], F32, kind="ExternalInput").ap()
    bv_h = nc.dram_tensor("bv", [D], F32, kind="ExternalInput").ap()
    bo_h = nc.dram_tensor("bo", [D], F32, kind="ExternalInput").ap()
    fb_h = nc.dram_tensor("fb", [D], F32, kind="ExternalInput").ap()
    sin_h = nc.dram_tensor("sin", [1], F32, kind="ExternalInput").ap()
    sw_h = nc.dram_tensor("sw", [1], F32, kind="ExternalInput").ap()
    y_h = nc.dram_tensor("y", [SQ, D], F32, kind="ExternalOutput").ap()
    wqb_h = nc.dram_tensor("wqb", [128, OC, OC, 128], BF16, kind="Internal").ap()
    wob_h = nc.dram_tensor("wob", [128, OQ, OC, 256], BF16, kind="Internal").ap()
    wfb_h = nc.dram_tensor("wfb", [128, OQ, OC, 256], BF16, kind="Internal").ap()
    if dbg:
        dbg_specs = [("d_msc", [128, KC]), ("d_lrow", [1, QB]), ("d_lcol", [128, 2]),
                     ("d_rl", [128, 2]), ("d_e0", [128, QB]), ("d_av0", [128, QB]),
                     ("d_y1", [128, D]), ("d_x2", [128, D]), ("d_k0", [128, 512]),
                     ("d_v0", [128, 512]), ("d_lf2", [128, QB]), ("d_xq", [128, D]),
                     ("d_t2", [128, D]), ("d_id", [128, 128]), ("d_q0", [128, QB]),
                     ("d_sqd", [128, D]), ("d_rt", [128, 2]), ("d_ext", [128, QB]),
                     ("d_wk", [128, D]), ("d_lntm", [128, D]), ("d_lf", [128, 512]),
                     ("d_rt1", [128, 2]), ("d_rec1", [128, 2]), ("d_mu1", [128, 2]),
                     ("d_sqd1", [128, D]), ("d_xt", [128, D])]
        dbg_h = {n: nc.dram_tensor(n, sh, F32, kind="ExternalOutput").ap() for n, sh in dbg_specs}


    import contextlib
    es = contextlib.ExitStack()
    with es:
        def sb(name, shape, dt=F32):
            return es.enter_context(nc.sbuf_tensor(name, shape, dt))

        def psum(name, shape, dt=F32):
            return es.enter_context(nc.psum_tensor(name, shape, dt))

        def sem(name):
            return es.enter_context(nc.semaphore(name))

        k_s = sb("k_s", [128, OC, S], BF16)
        v_s = sb("v_s", [128, KC, D], BF16)
        bo_bc = sb("bo_bc", [128, D], BF16)
        bq_t = sb("bq_t", [128, OC])
        bk_t = sb("bk_t", [128, OC])
        bv_row = sb("bv_row", [1, D], BF16)
        fb_div = sb("fb_div", [1, D], BF16)
        msc = sb("msc", [128, KC])
        ident_b = sb("ident_b", [128, 128], BF16)
        ones_row = sb("ones_row", [1, 128], BF16)
        ones_col = sb("ones_col", [128, 1], BF16)
        cst = sb("cst", [128, 4])
        sin_bc = sb("sin_bc", [128, 1])
        sw_bc = sb("sw_bc", [128, 1])
        stot_bc = sb("stot_bc", [128, 1])
        rsin_bc = sb("rsin_bc", [128, 1])
        rstot_bc = sb("rstot_bc", [128, 1])
        l_row = sb("l_row", [1, QB])
        l_col = sb("l_col", [128, NSUB])
        r_l = sb("r_l", [128, NSUB])
        musum = sb("musum", [128, 4])
        sqsum = sb("sqsum", [128, 4])
        mu_t = sb("mu_t", [128, 4])
        m2_t = sb("m2_t", [128, 4])
        var_t = sb("var_t", [128, 4])
        rec_t = sb("rec_t", [128, 4])
        r_t = sb("r_t", [128, 4])
        rp_t = sb("rp_t", [128, 4])
        bp_t = sb("bp_t", [128, 4])
        msq_t = sb("msq_t", [128, 4])
        fence_s = sb("fence_s", [1, 4], BF16)

        p_tr = psum("p_tr", [128, 8, 128], BF16)
        p_m = [psum("p_m0", [128, 512]), psum("p_m1", [128, 512])]
        p_s = [psum("p_s0", [128, 512]), psum("p_s1", [128, 512])]
        p_a = [psum("p_a0", [128, 512]), psum("p_a1", [128, 512])]
        p_l = psum("p_l", [1, QB])
        pool_kv = [p_m[0], p_m[1], p_s[0], p_s[1], p_a[0], p_a[1]]
        pool_q = [p_m[0], p_m[1], p_a[0], p_a[1]]
        pool_sc = [p_s[0], p_s[1], p_a[0], p_a[1], p_m[0], p_m[1]]
        pool_av = [p_a[0], p_a[1], p_s[0], p_s[1]]
        pool_pf = [p_m[0], p_m[1], p_s[0], p_s[1], p_a[0], p_a[1]]

        s_c = sem("s_c");   s_d0 = sem("s_d0"); s_a0 = sem("s_a0"); s_d0a = sem("s_d0a")
        s_w1 = sem("s_w1"); s_x = sem("s_x");   s_sq = sem("s_sq"); s_vr = sem("s_vr")
        s_r = sem("s_r");   s_ln = sem("s_ln"); s_trp = sem("s_trp"); s_tre = sem("s_tre")
        s_pp = sem("s_pp"); s_pe = sem("s_pe"); s_sp = sem("s_sp");  s_see = sem("s_see")
        s_sec = sem("s_sec"); s_lp = sem("s_lp"); s_le = sem("s_le"); s_lr = sem("s_lr")
        s_rc = sem("s_rc"); s_ap = sem("s_ap"); s_ae = sem("s_ae");  s_w2 = sem("s_w2")
        s_y1 = sem("s_y1")
        s_out = sem("s_out"); s_bar = sem("s_bar"); s_ci = sem("s_ci"); s_wk = sem("s_wk")
        s_bc = sem("s_bc"); s_pc = sem("s_pc"); s_w3 = sem("s_w3")

        block = es.enter_context(nc.Block())

        with contextlib.ExitStack() as p1:
            def sb1(name, shape, dt=F32):
                return p1.enter_context(nc.sbuf_tensor(name, shape, dt))

            wk_blk = sb1("wk_blk", [128, 4, OC, 128], BF16)
            wv_s = sb1("wv_s", [128, OC, D], BF16)
            x_t = sb1("x_t", [128, 4, D])
            ln_tm = sb1("ln_tm", [128, 3, D], BF16)
            lf = sb1("lf", [128, 2, OC, 512], BF16)
            io_kc = sb1("io_kc", [128, KC], I32)
            msf = sb1("msf", [128, KC])
            msz = sb1("msz", [128, KC])
            idrow = sb1("idrow", [128, 128])
            idcol = sb1("idcol", [128, 1])
            sqd = sb1("sqd", [128, 2, D])

            @block.gpsimd
            def _(g):
                g.dma_start(out=bo_bc[:, :], in_=bass.AP(bo_h.tensor, 0, [[0, 128], [1, D]])).then_inc(s_c, 16)
                g.dma_start(out=bq_t[:, :], in_=bass.AP(bq_h.tensor, 0, [[1, 128], [128, OC]]),
                            allow_slow_non_contiguous=True).then_inc(s_c, 16)
                g.dma_start(out=bk_t[:, :], in_=bass.AP(bk_h.tensor, 0, [[1, 128], [128, OC]]),
                            allow_slow_non_contiguous=True).then_inc(s_c, 16)
                g.dma_start(out=bv_row[:, :], in_=bass.AP(bv_h.tensor, 0, [[0, 1], [1, D]])).then_inc(s_c, 16)
                g.dma_start(out=sqd[0:1, 0, :], in_=bass.AP(fb_h.tensor, 0, [[0, 1], [1, D]])).then_inc(s_c, 16)
                g.dma_start(out=sin_bc[:, :], in_=bass.AP(sin_h.tensor, 0, [[0, 128], [1, 1]])).then_inc(s_c, 16)
                g.dma_start(out=sw_bc[:, :], in_=bass.AP(sw_h.tensor, 0, [[0, 128], [1, 1]])).then_inc(s_c, 16)
                g.iota(io_kc[:, :], [[128, KC]], channel_multiplier=1)
                g.iota(idrow[:, :], [[1, 128]], channel_multiplier=0, allow_small_or_imprecise_dtypes=True)
                g.iota(idcol[:, :], [[1, 1]], channel_multiplier=1, allow_small_or_imprecise_dtypes=True)
                g.drain().then_inc(s_ci, 1)
                g.dma_start(out=wv_s[:, 0:4, :], in_=wvt_h[:, 0:4, :]).then_inc(s_w1, 16)
                g.dma_start(out=wv_s[:, 4:8, :], in_=wvt_h[:, 4:8, :]).then_inc(s_w1, 16)
                pcast = [(dst, srcw, c) for dst, srcw in
                         ((wqb_h, wqt_h), (wob_h, wot_h), (wfb_h, wft_h))
                         for c in range(8)]
                for w in range(NW):
                    for oc in range(OC):
                        bi = w * 8 + oc
                        if bi >= 4:
                            bp2 = bi - 4
                            g.wait_ge(s_pp, (bp2 // 8) * 16 + (bp2 % 8) + 1)
                        g.dma_start(out=wk_blk[:, bi % 4, :, :],
                                    in_=wkt_h[:, oc, :, :]).then_inc(s_wk, 16)
                        if bi % 2 == 1 and bi // 2 < len(pcast):
                            dst, srcw, c = pcast[bi // 2]
                            g.dma_start(out=dst[c * 16:(c + 1) * 16, :, :, :],
                                        in_=srcw[c * 16:(c + 1) * 16, :, :, :]).then_inc(s_pc, 16)
                for j in range(NW * 4, len(pcast)):
                    dst, srcw, c = pcast[j]
                    g.dma_start(out=dst[c * 16:(c + 1) * 16, :, :, :],
                                in_=srcw[c * 16:(c + 1) * 16, :, :, :]).then_inc(s_pc, 16)
                g.dma_start(out=fence_s[0:1, 0:1], in_=wqb_h[127:128, OC - 1, OC - 1, 127:128]).then_inc(s_pc, 16)
                g.dma_start(out=fence_s[0:1, 1:2], in_=wob_h[127:128, OQ - 1, OC - 1, 255:256]).then_inc(s_pc, 16)
                g.dma_start(out=fence_s[0:1, 2:3], in_=wfb_h[127:128, OQ - 1, OC - 1, 255:256]).then_inc(s_pc, 16)
                g.wait_ge(s_pc, 16 * (len(pcast) + 3))
                g.wait_ge(s_c, NCONST)
                g.wait_ge(s_w1, 32)
                if dbg:
                    g.wait_ge(s_pe, 16)
                    for h, t in [(dbg_h["d_wk"], wv_s[:, 0, :]), (dbg_h["d_lntm"], ln_tm[:, 0, :]),
                                 (dbg_h["d_lf"], lf[:, 0, :]), (dbg_h["d_rt1"], r_t[:, :]),
                                 (dbg_h["d_rec1"], rec_t[:, :]), (dbg_h["d_mu1"], mu_t[:, :]),
                                 (dbg_h["d_sqd1"], sqd[:, :]), (dbg_h["d_xt"], x_t[:, 0, :])]:
                        g.dma_start(out=h[:, :], in_=t).then_inc(s_lr, 16)
                    g.wait_ge(s_lr, 16 * 8)
                g.nop().then_inc(s_bar, 1)

            @block.vector
            def _(v):
                v.wait_ge(s_c, NCONST)
                v.wait_ge(s_ci, 1)
                v.memset(ones_row[:, :], 1.0)
                v.memset(ones_col[:, :], 1.0)
                v.memset(cst[:, 0:1], -SHIFT)
                v.memset(cst[:, 1:2], 4.0 / S)
                v.memset(cst[:, 2:3], -2.0)
                v.memset(cst[:, 3:4], -0.5)
                v.tensor_copy(msf[:, :], io_kc[:, :])
                v.tensor_scalar(ident_b[:, :], idrow[:, :], idcol[:, :1], None, ALU.is_equal)
                v.tensor_scalar(stot_bc[:, :], sin_bc[:, :], sw_bc[:, :1], None, ALU.mult)
                v.reciprocal(rsin_bc[:, :], sin_bc[:, :])
                v.drain()
                v.reciprocal(rstot_bc[:, :], stot_bc[:, :])
                v.drain()
                v.tensor_scalar(fb_div[0:1, :], sqd[0:1, 0, :], rstot_bc[0:1, 0:1], None, ALU.mult)
                v.drain().then_inc(s_d0a, 1)
                v.wait_ge(s_a0, 1)
                v.tensor_scalar(msc[:, :], msz[:, :], float(D) ** -0.5, None, ALU.mult)
                v.drain().then_inc(s_d0, 1)
                def kv_evac(v, g):
                    w, j = g // 16, g % 16
                    v.wait_ge(s_pp, g + 1)
                    if j < 8:
                        v.tensor_scalar(k_s[:, j, w * 512:(w + 1) * 512], pool_kv[g % 6][:, :],
                                        bk_t[:, j:j + 1], None, ALU.add).then_inc(s_pe, 1)
                    else:
                        tc, oh = (j - 8) // 2, (j - 8) % 2
                        v.tensor_copy(v_s[:, w * TPW + tc, oh * 512:(oh + 1) * 512],
                                      pool_kv[g % 6][:, :]).then_inc(s_pe, 1)

                def lf_evac(v, T):
                    w3 = T // 4
                    v.wait_ge(s_trp, T + 1)
                    if w3 >= 1:
                        v.wait_ge(s_pp, 16 * w3)
                    v.tensor_copy(lf[:, w3 % 2, :, (T % 4) * 128:(T % 4 + 1) * 128],
                                  p_tr[:, :, :]).then_inc(s_tre, 1)

                for w in range(NW):
                    for t in range(TPW):
                        T = w * TPW + t
                        par = T % 4
                        sp = T % 2
                        v.wait_ge(s_x, 16 * (T + 1))
                        v.reduce_sum(musum[:, par:par + 1], x_t[:, T % 4, :], axis=AXX)
                        v.wait_ge(s_sq, T + 1)
                        v.reduce_sum(sqsum[:, par:par + 1], sqd[:, sp, :], axis=AXX).then_inc(s_vr, 1)
                        if par % 2 == 1:
                            sl = par - 1
                            if T >= 4:
                                v.wait_ge(s_r, T // 2 - 1)
                            v.drain()
                            v.tensor_scalar(mu_t[:, sl:sl + 2], musum[:, sl:sl + 2], inv_d, None, ALU.mult)
                            v.drain()
                            v.tensor_scalar(m2_t[:, sl:sl + 2], sqsum[:, sl:sl + 2], inv_d, None, ALU.mult)
                            v.tensor_tensor(msq_t[:, sl:sl + 2], mu_t[:, sl:sl + 2], mu_t[:, sl:sl + 2], ALU.mult)
                            v.drain()
                            v.tensor_tensor(var_t[:, sl:sl + 2], m2_t[:, sl:sl + 2], msq_t[:, sl:sl + 2], ALU.subtract)
                            v.drain()
                            v.tensor_scalar(var_t[:, sl:sl + 2], var_t[:, sl:sl + 2], 1e-5, None, ALU.add)
                            v.drain()
                            v.reciprocal(rec_t[:, sl:sl + 2], var_t[:, sl:sl + 2])
                            v.drain().then_inc(s_bc, 1)
                    def norm_one(t):
                        T = w * TPW + t
                        par = T % 4
                        if par % 2 == 0:
                            v.wait_ge(s_r, T // 2 + 1)
                            v.tensor_tensor(bp_t[:, par:par + 2], mu_t[:, par:par + 2],
                                            r_t[:, par:par + 2], ALU.mult)
                            v.drain()
                            v.tensor_scalar(bp_t[:, par:par + 2], bp_t[:, par:par + 2], -1.0, None, ALU.mult)
                            v.drain()
                        if T >= 3:
                            v.wait_ge(s_trp, T - 2)
                        v.tensor_scalar(ln_tm[:, T % 3, :], x_t[:, T % 4, :], r_t[:, par:par + 1],
                                        bp_t[:, par:par + 1], ALU.mult, ALU.add).then_inc(s_ln, 1)

                    for t in range(3):
                        norm_one(t)
                    if w >= 1:
                        for j in range(12):
                            kv_evac(v, (w - 1) * 16 + j)
                    lf_evac(v, w * TPW + 0)
                    if w >= 1:
                        kv_evac(v, (w - 1) * 16 + 12)
                    lf_evac(v, w * TPW + 1)
                    if w >= 1:
                        kv_evac(v, (w - 1) * 16 + 13)
                    lf_evac(v, w * TPW + 2)
                    if w >= 1:
                        kv_evac(v, (w - 1) * 16 + 14)
                        kv_evac(v, (w - 1) * 16 + 15)
                    norm_one(3)
                    lf_evac(v, w * TPW + 3)
                for j in range(16):
                    kv_evac(v, (NW - 1) * 16 + j)
                v.drain()
                v.nop().then_inc(s_bar, 1)

            @block.scalar
            def _(sc):
                sc.wait_ge(s_d0a, 1)
                sc.activation(msz[:, :], msf[:, :], AF.Square, bias=cst[:, 2:3], scale=cst[:, 1:2])
                sc.drain()
                sc.activation(msz[:, :], msz[:, :], AF.Exp, scale=cst[:, 3:4])
                sc.drain().then_inc(s_a0, 1)
                for STEP in range(NT + 1):
                    if STEP >= 1 and (STEP - 1) % 2 == 1:
                        T = STEP - 1
                        sl = (T % 4) - 1
                        sc.wait_ge(s_bc, T // 2 + 1)
                        sc.activation(r_t[:, sl:sl + 2], rec_t[:, sl:sl + 2], AF.Sqrt)
                        sc.drain().then_inc(s_r, 1)
                    if STEP < NT:
                        T = STEP
                        sc.wait_ge(s_x, 16 * (T + 1))
                        if T >= 2:
                            sc.wait_ge(s_vr, T - 1)
                        sc.activation(sqd[:, T % 2, :], x_t[:, T % 4, :], AF.Square).then_inc(s_sq, 1)
                sc.drain()
                sc.nop().then_inc(s_bar, 1)

            @block.sync
            def _(sy):
                for T in range(NT):
                    xs = T % 4
                    if T >= 4:
                        sy.wait_ge(s_ln, T - 3)
                    sy.dma_start(out=x_t[:, xs, :], in_=xb_h[T * 128:(T + 1) * 128, :]).then_inc(s_x, 16)
                sy.wait_ge(s_x, 16 * NT)
                sy.nop().then_inc(s_bar, 1)

            @block.tensor
            def _(tn):
                tn.wait_ge(s_d0, 1)
                tn.wait_ge(s_w1, 32)
                for w in range(NW):
                    for t in range(TPW):
                        T = w * TPW + t
                        par = T % 4
                        tn.wait_ge(s_ln, T + 1)
                        if T >= 1:
                            tn.wait_ge(s_tre, T)
                        for dc in range(OC):
                            ins = tn.transpose(p_tr[:, dc, :], ln_tm[:, T % 3, dc * 128:(dc + 1) * 128],
                                               ident_b[:, :])
                        ins.then_inc(s_trp, 1)
                    tn.wait_ge(s_tre, TPW * (w + 1))
                    for oc in range(OC):
                        gk = w * 16 + oc
                        bi = w * 8 + oc
                        tn.wait_ge(s_wk, 16 * (bi + 1))
                        if gk >= 6:
                            tn.wait_ge(s_pe, gk - 5)
                        for dc in range(OC):
                            ins = tn.matmul(pool_kv[gk % 6][:, :], wk_blk[:, bi % 4, dc, :],
                                            lf[:, w % 2, dc, :], start=(dc == 0), stop=(dc == OC - 1))
                        ins.then_inc(s_pp, 1)
                    for tc in range(TPW):
                        gv0 = w * 16 + 8 + tc * 2
                        if gv0 >= 6:
                            tn.wait_ge(s_pe, gv0 - 4)
                        for dc in range(OC):
                            tn.matmul(pool_kv[gv0 % 6][:, :], lf[:, w % 2, dc, tc * 128:(tc + 1) * 128],
                                      wv_s[:, dc, 0:512], start=(dc == 0), stop=False)
                            tn.matmul(pool_kv[(gv0 + 1) % 6][:, :], lf[:, w % 2, dc, tc * 128:(tc + 1) * 128],
                                      wv_s[:, dc, 512:1024], start=(dc == 0), stop=False)
                        tn.matmul(pool_kv[gv0 % 6][:, :], ones_row[0:1, :], bv_row[0:1, 0:512],
                                  start=False, stop=True).then_inc(s_pp, 1)
                        tn.matmul(pool_kv[(gv0 + 1) % 6][:, :], ones_row[0:1, :], bv_row[0:1, 512:1024],
                                  start=False, stop=True).then_inc(s_pp, 1)
                tn.drain()
                tn.nop().then_inc(s_bar, 1)

        with contextlib.ExitStack() as p2:
            def sb2(name, shape, dt=F32):
                return p2.enter_context(nc.sbuf_tensor(name, shape, dt))

            e_s = sb2("e_s", [128, KC, QB], BF16)
            x2 = sb2("x2", [128, NSUB, D])
            q_s = sb2("q_s", [128, OC, QB], BF16)
            lf2 = sb2("lf2", [128, OC, QB], BF16)
            y1 = sb2("y1", [128, NSUB, D])
            y2 = sb2("y2", [128, 2, 256])
            t2 = sb2("t2", [128, 512])
            sqd2 = sb2("sqd2", [128, 2, 512])
            xq_tm = sb2("xq_tm", [128, D], BF16)
            ex_t = sb2("ex_t", [128, 3, QB])
            q8a = sb2("q8a", [128, 512], FP8)
            q8b = sb2("q8b", [128, 512], FP8)
            mB = sb2("mB", [128, 512], BF16)
            mask8 = sb2("mask8", [128, 512], U8)
            wq_blk = sb2("wq_blk", [128, 4, OC, 128], BF16)
            w2q = sb2("w2q", [128, 2, OC, 256], BF16)

            SEMC = {}
            def cg(s):
                return SEMC.get(id(s), 0)
            def ca(s, n):
                SEMC[id(s)] = cg(s) + n
                return SEMC[id(s)]
            for s_, v_ in [(s_pp, NPP1), (s_pe, NPP1), (s_trp, NTR1), (s_tre, NTR1),
                           (s_ln, NT), (s_sq, NT), (s_vr, NT), (s_r, NT // 2),
                           (s_x, 16 * NT)]:
                SEMC[id(s_)] = v_
            EV = {k: [] for k in ("g", "sy", "sc", "v", "tn")}
            def ev(eng, fn, waits=(), inc=None, n=1):
                EV[eng].append((fn, list(waits), inc, n))
                return ca(inc, n) if inc is not None else None

            def E_lnqT(qb, qs):
                def fn(tn, qs=qs):
                    for dc in range(OC):
                        ins = tn.transpose(p_tr[:, dc, :], xq_tm[:, dc * 128:(dc + 1) * 128], ident_b[:, :])
                    return ins
                return fn

            def E_tr_evac(dst, qs):
                def fn(v, dst=dst, qs=qs):
                    return v.tensor_copy(dst[:, :, qs * 128:(qs + 1) * 128], p_tr[:, :, :])
                return fn

            def E_qproj(qb, oc):
                def fn(tn, qb=qb, oc=oc):
                    bi = qb * OC + oc
                    for dc in range(OC):
                        ins = tn.matmul(pool_q[oc % 4][:, 0:QB], wq_blk[:, bi % 4, dc, :],
                                        lf2[:, dc, :], start=(dc == 0), stop=(dc == OC - 1))
                    return ins
                return fn

            def E_qevac(qb, oc):
                def fn(v, oc=oc):
                    return v.tensor_scalar(q_s[:, oc, :], pool_q[oc % 4][:, 0:QB],
                                           bq_t[:, oc:oc + 1], None, ALU.add)
                return fn

            def E_scores(qb, kc):
                def fn(tn, kc=kc):
                    for oc in range(OC):
                        ins = tn.matmul(pool_sc[kc % 6][:, 0:QB], k_s[:, oc, kc * 128:(kc + 1) * 128],
                                        q_s[:, oc, :], start=(oc == 0), stop=(oc == OC - 1))
                    return ins
                return fn

            def E_exp(qb, kc):
                def fn(sc, kc=kc):
                    return sc.activation(ex_t[:, kc % 3, :], pool_sc[kc % 6][:, 0:QB], AF.Exp,
                                         bias=cst[:, 0:1], scale=msc[:, kc:kc + 1])
                return fn

            def E_cast(qb, kc):
                def fn(v, kc=kc):
                    return v.tensor_copy(e_s[:, kc, :], ex_t[:, kc % 3, :])
                return fn

            def E_lrow(qb):
                def fn(tn):
                    for kc in range(KC):
                        ins = tn.matmul(p_l[0:1, :], ones_col[:, :], e_s[:, kc, :],
                                        start=(kc == 0), stop=(kc == KC - 1))
                    return ins
                return fn

            def E_av(qb, dc):
                def fn(tn, dc=dc):
                    for kc in range(KC):
                        ins = tn.matmul(pool_av[dc % 4][:, 0:QB], v_s[:, kc, dc * 128:(dc + 1) * 128],
                                        e_s[:, kc, :], start=(kc == 0), stop=(kc == KC - 1))
                    return ins
                return fn

            def E_avevac(qb, dc):
                def fn(v, dc=dc):
                    return v.tensor_copy(q_s[:, dc, :], pool_av[dc % 4][:, 0:QB])
                return fn

            def E_proj(qb, oq, qs):
                def fn(tn, qb=qb, oq=oq, qs=qs):
                    QI = qb * 8 + oq
                    for dc in range(OC):
                        ins = tn.matmul(pool_pf[(oq * 2 + qs) % 6][:, 0:256],
                                        q_s[:, dc, qs * 128:(qs + 1) * 128],
                                        w2q[:, QI % 2, dc, :], start=(dc == 0), stop=(dc == OC - 1))
                    return ins
                return fn

            def E_projstt(qb, oq, qs):
                def fn(v, oq=oq, qs=qs):
                    return v.scalar_tensor_tensor(y1[:, qs, oq * 256:(oq + 1) * 256],
                                                  pool_pf[(oq * 2 + qs) % 6][:, 0:256], r_l[:, qs:qs + 1],
                                                  y1[:, qs, oq * 256:(oq + 1) * 256], ALU.mult, ALU.add)
                return fn

            def E_xqT(qb, qs):
                def fn(tn):
                    for dc in range(OC):
                        ins = tn.transpose(p_tr[:, dc, :], xq_tm[:, dc * 128:(dc + 1) * 128], ident_b[:, :])
                    return ins
                return fn

            def E_ffn(qb, oq, qs):
                def fn(tn, qb=qb, oq=oq, qs=qs):
                    QI = qb * 8 + 4 + oq
                    for dc in range(OC):
                        tn.matmul(pool_pf[(oq * 2 + qs) % 6][:, 0:256], lf2[:, dc, qs * 128:(qs + 1) * 128],
                                  w2q[:, QI % 2, dc, :], start=(dc == 0), stop=False)
                    return tn.matmul(pool_pf[(oq * 2 + qs) % 6][:, 0:256], ones_row[0:1, :],
                                     fb_div[0:1, oq * 256:(oq + 1) * 256], start=False, stop=True)
                return fn

            def E_ffnstt(qb, oq, qs):
                def fn(v, oq=oq, qs=qs):
                    yg = oq * 2 + qs
                    return v.scalar_tensor_tensor(y2[:, yg % 2, :], pool_pf[(oq * 2 + qs) % 6][:, 0:256],
                                                  stot_bc[:, :1], y1[:, qs, oq * 256:(oq + 1) * 256],
                                                  ALU.mult, ALU.add)
                return fn

            def E_y2dma(qb, oq, qs):
                def fn(sy, qb=qb, oq=oq, qs=qs):
                    r0 = qb * QB + qs * 128
                    return sy.dma_start(out=y_h[r0:r0 + 128, oq * 256:(oq + 1) * 256],
                                        in_=y2[:, (oq * 2 + qs) % 2, :])
                return fn

            def E_sq(src_ap_fn, h):
                def fn(sc, h=h):
                    return sc.activation(sqd2[:, h, :], src_ap_fn()[:, h * 512:(h + 1) * 512], AF.Square)
                return fn

            def E_sqv(src_ap_fn, h):
                def fn(v, h=h):
                    s = src_ap_fn()[:, h * 512:(h + 1) * 512]
                    return v.tensor_tensor(sqd2[:, h, :], s, s, ALU.mult)
                return fn

            def E_red(src_ap_fn, slot):
                def fn(v, slot=slot):
                    v.reduce_sum(musum[:, slot:slot + 1], src_ap_fn(), axis=AXX)
                    v.reduce_sum(sqsum[:, slot:slot + 1], sqd2[:, 0, :], axis=AXX)
                    v.reduce_sum(msq_t[:, slot:slot + 1], sqd2[:, 1, :], axis=AXX)
                    v.drain()
                    v.tensor_scalar(sqsum[:, slot:slot + 1], sqsum[:, slot:slot + 1],
                                    msq_t[:, slot:slot + 1], None, ALU.add)
                    v.drain()
                    v.tensor_scalar(mu_t[:, slot:slot + 1], musum[:, slot:slot + 1], inv_d, None, ALU.mult)
                    v.drain()
                    v.tensor_scalar(m2_t[:, slot:slot + 1], sqsum[:, slot:slot + 1], inv_d, None, ALU.mult)
                    v.drain()
                    v.tensor_scalar(msq_t[:, slot:slot + 1], mu_t[:, slot:slot + 1],
                                    mu_t[:, slot:slot + 1], None, ALU.mult)
                    v.drain()
                    v.tensor_scalar(var_t[:, slot:slot + 1], m2_t[:, slot:slot + 1],
                                    msq_t[:, slot:slot + 1], 1e-5, ALU.subtract, ALU.add)
                    v.drain()
                    v.reciprocal(rec_t[:, slot:slot + 1], var_t[:, slot:slot + 1])
                    return v.drain()
                return fn

            def E_sqrt(slot):
                def fn(sc, slot=slot):
                    sc.activation(r_t[:, slot:slot + 1], rec_t[:, slot:slot + 1], AF.Sqrt)
                    return sc.drain()
                return fn

            def E_bp1(slot):
                def fn(v, slot=slot):
                    v.tensor_scalar(bp_t[:, slot:slot + 1], mu_t[:, slot:slot + 1],
                                    r_t[:, slot:slot + 1], -1.0, ALU.mult, ALU.mult)
                    return v.drain()
                return fn

            def E_norm1(qs):
                def fn(v, qs=qs):
                    return v.tensor_scalar(xq_tm[:, :], x2[:, qs, :], r_t[:, qs:qs + 1],
                                           bp_t[:, qs:qs + 1], ALU.mult, ALU.add)
                return fn

            def E_bp2(slot):
                def fn(v, slot=slot):
                    v.tensor_scalar(rp_t[:, slot:slot + 1], r_t[:, slot:slot + 1],
                                    rsin_bc[:, :1], None, ALU.mult)
                    v.drain()
                    v.tensor_scalar(bp_t[:, slot:slot + 1], mu_t[:, slot:slot + 1],
                                    rp_t[:, slot:slot + 1], -1.0, ALU.mult, ALU.mult)
                    return v.drain()
                return fn

            def E_norm2h(qs, h):
                def fn(v, qs=qs, h=h):
                    v.tensor_scalar(t2[:, :], y1[:, qs, h * 512:(h + 1) * 512], rp_t[:, qs:qs + 1],
                                    bp_t[:, qs:qs + 1], ALU.mult, ALU.add)
                    return v.tensor_scalar(t2[:, :], t2[:, :], FP8_MAX, -FP8_MAX, ALU.min, ALU.max)
                return fn

            def E_quant_h(qs, h, part):
                def fn(v, qs=qs, h=h, part=part):
                    xs = xq_tm[:, h * 512:(h + 1) * 512]
                    if part == 0:
                        v.tensor_copy(q8a[:, :], t2[:, :])
                        v.tensor_scalar(q8b[:, :], t2[:, :], 0.5, None, ALU.mult)
                        return v.tensor_copy(xs, q8a[:, :])
                    else:
                        v.tensor_scalar(mB[:, :], q8b[:, :], 2.0, None, ALU.mult)
                        v.tensor_scalar(mask8[:, :], t2[:, :], float(HALF_SMALL), None, ALU.is_ge)
                        v.copy_predicated(xs, mask8[:, :], mB[:, :])
                        v.tensor_scalar(mask8[:, :], t2[:, :], float(-HALF_SMALL), None, ALU.is_le)
                        return v.copy_predicated(xs, mask8[:, :], mB[:, :])
                return fn

            def E_pred(oq, qs):
                def fn(v, oq=oq, qs=qs):
                    pm = musum if qs == 0 else sqsum
                    ps = rp_t if qs == 0 else bp_t
                    sl = y1[:, qs, oq * 256:(oq + 1) * 256]
                    v.reduce_sum(pm[:, oq:oq + 1], sl, axis=AXX)
                    v.tensor_tensor(t2[:, 0:256], sl, sl, ALU.mult)
                    v.drain()
                    return v.reduce_sum(ps[:, oq:oq + 1], t2[:, 0:256], axis=AXX)
                return fn

            def E_redp(qs):
                def fn(v, qs=qs):
                    pm = musum if qs == 0 else sqsum
                    ps = rp_t if qs == 0 else bp_t
                    v.reduce_sum(var_t[:, qs:qs + 1], pm[:, 0:4], axis=AXX)
                    v.reduce_sum(rec_t[:, qs:qs + 1], ps[:, 0:4], axis=AXX)
                    v.drain()
                    v.tensor_scalar(mu_t[:, qs:qs + 1], var_t[:, qs:qs + 1], inv_d, None, ALU.mult)
                    v.drain()
                    v.tensor_scalar(m2_t[:, qs:qs + 1], rec_t[:, qs:qs + 1], inv_d, None, ALU.mult)
                    v.drain()
                    v.tensor_scalar(msq_t[:, qs:qs + 1], mu_t[:, qs:qs + 1],
                                    mu_t[:, qs:qs + 1], None, ALU.mult)
                    v.drain()
                    v.tensor_scalar(var_t[:, qs:qs + 1], m2_t[:, qs:qs + 1],
                                    msq_t[:, qs:qs + 1], 1e-5, ALU.subtract, ALU.add)
                    v.drain()
                    v.reciprocal(rec_t[:, qs:qs + 1], var_t[:, qs:qs + 1])
                    return v.drain()
                return fn

            def E_x2load(qb, qs):
                def fn(sy, qb=qb, qs=qs):
                    r0 = qb * QB + qs * 128
                    return sy.dma_start(out=x2[:, qs, :], in_=xo_h[r0:r0 + 128, :])
                return fn

            def E_lcopy(qb):
                def fn(sc):
                    sc.copy(l_row[0:1, :], p_l[0:1, :])
                    return sc.drain()
                return fn

            def E_redist(qb, ssub):
                def fn(g, ssub=ssub):
                    return g.dma_start(out=l_col[:, ssub:ssub + 1],
                                       in_=l_row[0:1, ssub * 128:(ssub + 1) * 128],
                                       allow_slow_non_contiguous=True)
                return fn

            def E_recip(qb):
                def fn(v):
                    v.reciprocal(r_l[:, :], l_col[:, :])
                    return v.drain()
                return fn

            def E_y1pre(qb, qs):
                def fn(v, qs=qs):
                    return v.tensor_add(y1[:, qs, :], x2[:, qs, :], bo_bc[:, :])
                return fn

            def E_wq(qb, oc):
                def fn(g, qb=qb, oc=oc):
                    bi = qb * OC + oc
                    return g.dma_start(out=wq_blk[:, bi % 4, :, :],
                                       in_=wqb_h[:, oc, :, :])
                return fn

            def E_w2(qb, j, which):
                def fn(g, qb=qb, j=j, which=which):
                    QI = qb * 8 + (0 if which == "o" else 4) + j
                    srcr = wob_h if which == "o" else wfb_h
                    return g.dma_start(out=w2q[:, QI % 2, :, :], in_=srcr[:, j, :, :])
                return fn

            qevac_done = {}
            lnq_done = {}
            quant_done = {}
            lnqT_done = {}
            xqT_done = {}
            cast_done = {}
            exp_done = {}
            scg = {}
            avp = {}
            ave_done = {}
            ppg = {}
            peg = {}
            wqc = {}
            w2c = {}
            y1_done = {}
            lr_c = {}
            le_c = {}
            sqc = {}
            rc = {}
            vrc = {}
            outc = {}

            def sched_LN1_sq(qb, qs):
                ev("v", E_sqv(lambda qb=qb, qs=qs: x2[:, qs, :], 0),
                   waits=[(s_x, 16 * NT + 16 * (qb * 2 + qs + 1))])
                ev("v", E_sqv(lambda qb=qb, qs=qs: x2[:, qs, :], 1))

            def sched_LN1_red(qb, qs):
                k = ("ln1", qb, qs)
                vrc[k] = ev("v", E_red(lambda qb=qb, qs=qs: x2[:, qs, :], qs), inc=s_vr)
                rc[k] = ev("sc", E_sqrt(qs), waits=[(s_vr, vrc[k])], inc=s_r)

            def sched_LN1_norm(qb, qs):
                k = ("ln1", qb, qs)
                ev("v", E_bp1(qs), waits=[(s_r, rc[k])])
                tw = cg(s_trp) + 1 if qs == 1 else cg(s_trp)
                lnq_done[(qb, qs)] = ev("v", E_norm1(qs), waits=[(s_trp, tw)], inc=s_ln)

            def sched_LN1(qb):
                for qs in range(NSUB):
                    sched_LN1_sq(qb, qs)
                    sched_LN1_red(qb, qs)
                    sched_LN1_norm(qb, qs)

            def sched_quant_block(qb, qs, part):
                k = ("ln2", qb, qs)
                if part == 0:
                    ev("v", E_sqv(lambda qb=qb, qs=qs: y1[:, qs, :], 0))
                    ev("v", E_sqv(lambda qb=qb, qs=qs: y1[:, qs, :], 1))
                elif part == 1:
                    vrc[k] = ev("v", E_red(lambda qb=qb, qs=qs: y1[:, qs, :], qs), inc=s_vr)
                    rc[k] = ev("sc", E_sqrt(qs), waits=[(s_vr, vrc[k])], inc=s_r)
                elif part == 2:
                    ev("v", E_bp2(qs), waits=[(s_r, rc[k])])
                    ev("v", E_norm2h(qs, 0))
                elif part == 3:
                    ev("v", E_quant_h(qs, 0, 0), waits=[(s_trp, cg(s_trp))])
                    ev("v", E_quant_h(qs, 0, 1))
                elif part == 4:
                    ev("v", E_norm2h(qs, 1))
                    ev("v", E_quant_h(qs, 1, 0))
                elif part == 5:
                    quant_done[(qb, qs)] = ev("v", E_quant_h(qs, 1, 1), inc=s_ln)

            for e in ("g", "sy", "sc", "v", "tn"):
                EV[e].append((None, [(s_bar, 5)], None, 1))
            for qs in range(NSUB):
                ev("sy", E_x2load(0, qs), inc=s_x, n=16)
            for oc in range(4):
                wqc[(0, oc)] = ev("g", E_wq(0, oc), inc=s_w2, n=16)
            sched_LN1(0)

            def emit_ffn(fqb, in_att, qb):
                for oq in range(OQ):
                    for qs in range(NSUB):
                        fi = oq * 2 + qs
                        w = [(s_tre, cg(s_tre)), (s_w2, w2c[fqb * 8 + 4 + oq])]
                        if fi >= 6:
                            w.append((s_pe, peg[("f", fqb, 0, fi - 6)]))
                        elif in_att and 2 <= fi <= 3:
                            w.append((s_ae, peg[("av", qb, 4 + fi)]))
                        elif in_att and 4 <= fi <= 5:
                            w.append((s_ae, peg[("av", qb, fi)]))
                        elif not in_att:
                            pmap = {0: (3, 0), 1: (3, 1), 2: (1, 0), 3: (1, 1),
                                    4: (2, 0), 5: (2, 1)}
                            w.append((s_pe, peg[("pr", fqb) + pmap[fi]]))
                        ppg[("f", fqb, oq, qs)] = ev("tn", E_ffn(fqb, oq, qs), waits=w, inc=s_pp)
                        if qs == 1 and oq < 2:
                            w2c[fqb * 8 + 4 + oq + 2] = ev("g", E_w2(fqb, oq + 2, "f"),
                                                           waits=[(s_pp, ppg[("f", fqb, oq, 1)])],
                                                           inc=s_w2, n=16)
                        if in_att and qs == 1 and oq >= 2:
                            w2c[qb * 8 + (oq - 2)] = ev("g", E_w2(qb, oq - 2, "o"),
                                                        waits=[(s_pp, ppg[("f", fqb, oq, 1)])],
                                                        inc=s_w2, n=16)
                        wv_ = [(s_pp, ppg[("f", fqb, oq, qs)])]
                        og = outc.get((fqb, oq - 1, qs))
                        if og is not None:
                            wv_.append((s_out, og))
                        peg[("f", fqb, oq, qs)] = ev("v", E_ffnstt(fqb, oq, qs),
                                                     waits=wv_, inc=s_pe)
                        outc[(fqb, oq, qs)] = ev("sy", E_y2dma(fqb, oq, qs),
                                                 waits=[(s_pe, peg[("f", fqb, oq, qs)])],
                                                 inc=s_out, n=16)
                        if in_att and fi in (6, 7):
                            pqs = fi - 6
                            y1_done[(qb, pqs)] = ev("v", E_y1pre(qb, pqs), inc=s_y1)
                            if qb + 1 < NQB:
                                ev("sy", E_x2load(qb + 1, pqs),
                                   waits=[(s_y1, y1_done[(qb, pqs)])], inc=s_x, n=16)
                                if fi == 7:
                                    sched_LN1_sq(qb + 1, 0)
                                    sched_LN1_red(qb + 1, 0)

            for qb in range(NQB + 1):
                att = qb < NQB
                tail = qb >= 1
                if att:
                    for qs in range(NSUB):
                        lnqT_done[(qb, qs)] = ev("tn", E_lnqT(qb, qs),
                                                 waits=[(s_ln, lnq_done[(qb, qs)]), (s_tre, cg(s_tre))],
                                                 inc=s_trp)
                        ev("v", E_tr_evac(lf2, qs), waits=[(s_trp, lnqT_done[(qb, qs)]),
                                                           (s_pp, cg(s_pp))], inc=s_tre)
                    for oc in range(OC):
                        w = [(s_w2, wqc[(qb, oc)]), (s_tre, cg(s_tre))]
                        if oc >= 4:
                            w.append((s_pe, peg[("q", qb, oc - 4)]))
                        elif qb >= 1:
                            qmap = {0: (3, 0), 1: (3, 1), 2: (2, 0), 3: (2, 1)}
                            w.append((s_pe, peg[("pr", qb - 1) + qmap[oc]]))
                        ppg[("q", qb, oc)] = ev("tn", E_qproj(qb, oc), waits=w, inc=s_pp)
                        nb, no = divmod(qb * 8 + oc + 4, 8)
                        if nb < NQB:
                            wqc[(nb, no)] = ev("g", E_wq(nb, no),
                                               waits=[(s_pp, ppg[("q", qb, oc)])], inc=s_w2, n=16)
                        peg[("q", qb, oc)] = ev("v", E_qevac(qb, oc),
                                                waits=[(s_pp, ppg[("q", qb, oc)])], inc=s_pe)
                    qevac_done[qb] = cg(s_pe)
                    qsched = {}
                    if tail:
                        for pos, jj in [(2, 0), (9, 1), (16, 6), (23, 7)]:
                            pos = min(max(1, pos * KC // 32), KC - 1)
                            qsched.setdefault(pos, []).append(jj)
                    for kc in range(KC):
                        w = [(s_pe, qevac_done[qb])]
                        if kc >= 6:
                            w.append((s_see, exp_done[(qb, kc - 6)]))
                        if qb >= 1 and kc < 6:
                            prevkc = kc + 6 * ((KC - 1 - kc) // 6)
                            w.append((s_see, exp_done[(qb - 1, prevkc)]))
                        scg[(qb, kc)] = ev("tn", E_scores(qb, kc), waits=w, inc=s_sp)
                        we = [(s_sp, scg[(qb, kc)])]
                        if kc >= 3:
                            we.append((s_sec, cast_done[(qb, kc - 3)]))
                        elif qb >= 1:
                            prevkc = kc + 3 * ((KC - 1 - kc) // 3)
                            we.append((s_sec, cast_done[(qb - 1, prevkc)]))
                        exp_done[(qb, kc)] = ev("sc", E_exp(qb, kc), waits=we, inc=s_see)
                        wc = [(s_see, exp_done[(qb, kc)])]
                        if qb >= 1 and kc == 0:
                            wc.append((s_ap, avp[(qb - 1, 7)]))
                        cast_done[(qb, kc)] = ev("v", E_cast(qb, kc), waits=wc, inc=s_sec)
                        for jj in qsched.get(kc, []):
                            sched_quant_block(qb - 1, jj // 6, jj % 6)
                    ev("tn", E_lrow(qb), waits=[(s_sec, cast_done[(qb, KC - 1)]),
                                                (s_le, le_c.get(qb - 1, 0))], inc=s_lp)
                    le_c[qb] = ev("sc", E_lcopy(qb), waits=[(s_lp, cg(s_lp)),
                                                            (s_lr, lr_c.get(qb - 1, 0))], inc=s_le)
                    for ssub in range(NSUB):
                        lr_c[qb] = ev("g", E_redist(qb, ssub),
                                      waits=([(s_le, le_c[qb])] if ssub == 0 else []) +
                                            ([(s_rc, cg(s_rc))] if ssub == 0 else []),
                                      inc=s_lr, n=16)
                    for dc in range(OC):
                        w = [(s_sec, cast_done[(qb, KC - 1)])]
                        if dc >= 4:
                            w.append((s_ae, peg[("av", qb, dc - 4)]))
                        avp[(qb, dc)] = ev("tn", E_av(qb, dc), waits=w, inc=s_ap)
                        peg[("av", qb, dc)] = ev("v", E_avevac(qb, dc),
                                                 waits=[(s_ap, avp[(qb, dc)])], inc=s_ae)
                        if tail:
                            if dc <= 2:
                                sched_quant_block(qb - 1, 0, dc + 2)
                            elif dc == 3:
                                sched_quant_block(qb - 1, 0, 5)
                                sched_quant_block(qb - 1, 1, 2)
                            elif dc == 4:
                                xqT_done[(qb - 1, 0)] = ev("tn", E_xqT(qb - 1, 0),
                                                           waits=[(s_ln, quant_done[(qb - 1, 0)]),
                                                                  (s_tre, cg(s_tre))], inc=s_trp)
                                ev("v", E_tr_evac(lf2, 0), waits=[(s_trp, xqT_done[(qb - 1, 0)]),
                                                                  (s_pp, cg(s_pp))], inc=s_tre)
                                sched_quant_block(qb - 1, 1, 3)
                            elif dc == 5:
                                sched_quant_block(qb - 1, 1, 4)
                            elif dc == 6:
                                sched_quant_block(qb - 1, 1, 5)
                        if dc == 6:
                            ev("v", E_recip(qb), waits=[(s_lr, lr_c[qb])], inc=s_rc)
                    ave_done[qb] = cg(s_ae)
                    if tail:
                        xqT_done[(qb - 1, 1)] = ev("tn", E_xqT(qb - 1, 1),
                                                   waits=[(s_ln, quant_done[(qb - 1, 1)]),
                                                          (s_tre, cg(s_tre))], inc=s_trp)
                        ev("v", E_tr_evac(lf2, 1), waits=[(s_trp, xqT_done[(qb - 1, 1)]),
                                                          (s_pp, cg(s_pp))], inc=s_tre)
                        emit_ffn(qb - 1, True, qb)
                    else:
                        for qs in range(NSUB):
                            y1_done[(qb, qs)] = ev("v", E_y1pre(qb, qs), inc=s_y1)
                            if qb + 1 < NQB:
                                ev("sy", E_x2load(qb + 1, qs), waits=[(s_y1, y1_done[(qb, qs)])],
                                   inc=s_x, n=16)
                        if qb + 1 < NQB:
                            sched_LN1_sq(qb + 1, 0)
                            sched_LN1_red(qb + 1, 0)
                        for j in range(2):
                            w2c[qb * 8 + j] = ev("g", E_w2(qb, j, "o"), inc=s_w2, n=16)
                    for oq in range(OQ):
                        for qs in range(NSUB):
                            w = [(s_ae, ave_done[qb]), (s_w2, w2c[qb * 8 + oq])]
                            pi = oq * 2 + qs
                            if pi >= 6:
                                w.append((s_pe, peg[("pr", qb, 0, pi - 6)]))
                            elif qb >= 1:
                                fmap = {0: (3, 0), 1: (3, 1), 2: (1, 0), 3: (1, 1),
                                        4: (2, 0), 5: (2, 1)}
                                w.append((s_pe, peg[("f", qb - 1) + fmap[pi]]))
                            ppg[("pr", qb, oq, qs)] = ev("tn", E_proj(qb, oq, qs), waits=w, inc=s_pp)
                            peg[("pr", qb, oq, qs)] = ev("v", E_projstt(qb, oq, qs),
                                                         waits=[(s_pp, ppg[("pr", qb, oq, qs)])], inc=s_pe)
                            if qb + 1 < NQB:
                                if (oq, qs) == (0, 1):
                                    sched_LN1_sq(qb + 1, 1)
                                elif (oq, qs) == (1, 1):
                                    sched_LN1_red(qb + 1, 1)
                                elif (oq, qs) == (3, 0):
                                    sched_LN1_norm(qb + 1, 0)
                            else:
                                ev("v", E_pred(oq, qs))
                        if oq < 2:
                            w2c[qb * 8 + oq + 2] = ev("g", E_w2(qb, oq + 2, "o"),
                                                      waits=[(s_pp, ppg[("pr", qb, oq, 1)])],
                                                      inc=s_w2, n=16)
                        else:
                            w2c[qb * 8 + 4 + (oq - 2)] = ev("g", E_w2(qb, oq - 2, "f"),
                                                            waits=[(s_pp, ppg[("pr", qb, oq, 1)])],
                                                            inc=s_w2, n=16)
                    if qb + 1 < NQB:
                        sched_LN1_norm(qb + 1, 1)
                else:
                    k0 = ("ln2", qb - 1, 0)
                    k1 = ("ln2", qb - 1, 1)
                    vrc[k0] = ev("v", E_redp(0), inc=s_vr)
                    vrc[k1] = ev("v", E_redp(1), inc=s_vr)
                    rc[k0] = ev("sc", E_sqrt(0), waits=[(s_vr, vrc[k0])], inc=s_r)
                    rc[k1] = ev("sc", E_sqrt(1), waits=[(s_vr, vrc[k1])], inc=s_r)
                    for jj in range(2, 6):
                        sched_quant_block(qb - 1, 0, jj)
                    xqT_done[(qb - 1, 0)] = ev("tn", E_xqT(qb - 1, 0),
                                               waits=[(s_ln, quant_done[(qb - 1, 0)]),
                                                      (s_tre, cg(s_tre))], inc=s_trp)
                    ev("v", E_tr_evac(lf2, 0), waits=[(s_trp, xqT_done[(qb - 1, 0)]),
                                                      (s_pp, cg(s_pp))], inc=s_tre)
                    for jj in range(2, 6):
                        sched_quant_block(qb - 1, 1, jj)
                    xqT_done[(qb - 1, 1)] = ev("tn", E_xqT(qb - 1, 1),
                                               waits=[(s_ln, quant_done[(qb - 1, 1)]),
                                                      (s_tre, cg(s_tre))], inc=s_trp)
                    ev("v", E_tr_evac(lf2, 1), waits=[(s_trp, xqT_done[(qb - 1, 1)]),
                                                      (s_pp, cg(s_pp))], inc=s_tre)
                    emit_ffn(qb - 1, False, qb)
            EV["sy"].append((None, [(s_out, 16 * 8 * NQB)], None, 1))

            def run_events(e, key):
                for fn, waits, inc, n in EV[key]:
                    for s_, val in waits:
                        if val > 0:
                            e.wait_ge(s_, val)
                    if fn is not None:
                        ins = fn(e)
                        if inc is not None:
                            ins.then_inc(inc, n)

            @block.gpsimd
            def _(g):
                run_events(g, "g")

            @block.sync
            def _(sy):
                run_events(sy, "sy")

            @block.scalar
            def _(sc):
                run_events(sc, "sc")

            @block.vector
            def _(v):
                run_events(v, "v")

            @block.tensor
            def _(tn):
                run_events(tn, "tn")

    return nc


def kernel(L0_attn_x, L0_attn_Wq, L0_attn_bq, L0_attn_Wk, L0_attn_bk,
           L0_attn_Wv, L0_attn_bv, L0_attn_Wo, L0_attn_bo,
           L0_ffn_weight_fp8, L0_ffn_bias, L0_ffn_input_scale, L0_ffn_weight_scale):
    x = np.ascontiguousarray(np.asarray(L0_attn_x, dtype=np.float32))
    B, S, D = x.shape
    SQ = S // 2
    OC = D // 128
    OQ = D // 256

    def blk4(w, cols):
        nb = D // cols
        return np.ascontiguousarray(
            np.asarray(w, np.float32).T.reshape(OC, 128, nb, cols).transpose(1, 2, 0, 3))

    wqt = blk4(L0_attn_Wq, 128)
    wkt = blk4(L0_attn_Wk, 128)
    wvt = np.ascontiguousarray(
        np.asarray(L0_attn_Wv, np.float32).T.reshape(OC, 128, D).transpose(1, 0, 2))
    wot = blk4(L0_attn_Wo, 256)
    wft = blk4(L0_ffn_weight_fp8, 256)
    bq = np.ascontiguousarray(np.asarray(L0_attn_bq, np.float32))
    bk = np.ascontiguousarray(np.asarray(L0_attn_bk, np.float32))
    bv = np.ascontiguousarray(np.asarray(L0_attn_bv, np.float32))
    bo = np.ascontiguousarray(np.asarray(L0_attn_bo, np.float32))
    fb = np.ascontiguousarray(np.asarray(L0_ffn_bias, np.float32))
    sin = np.ascontiguousarray(np.asarray(L0_ffn_input_scale, np.float32))
    sw = np.ascontiguousarray(np.asarray(L0_ffn_weight_scale, np.float32))

    nc = build_nc(S=S, SQ=SQ, D=D, QB=256)
    in_maps = []
    for c in range(8):
        b, h = c // 2, c % 2
        in_maps.append({
            "xb": np.ascontiguousarray(x[b]),
            "xo": np.ascontiguousarray(x[b, h * SQ:(h + 1) * SQ]),
            "wqt": wqt, "wkt": wkt, "wvt": wvt, "wot": wot, "wft": wft,
            "bq": bq, "bk": bk, "bv": bv, "bo": bo, "fb": fb,
            "sin": sin, "sw": sw,
        })
    res = run_bass_kernel_spmd(nc, in_maps, core_ids=list(range(8)))
    out = np.empty((B, S, D), dtype=np.float32)
    for c in range(8):
        b, h = c // 2, c % 2
        out[b, h * SQ:(h + 1) * SQ] = res.results[c]["y"]
    return out


# version 39
# speedup vs baseline: 1.0067x; 1.0067x over previous
import numpy as np
import concourse.bass as bass
import concourse.mybir as mybir
from concourse.bass_utils import run_bass_kernel_spmd

F32 = mybir.dt.float32
BF16 = mybir.dt.bfloat16
FP8 = mybir.dt.float8e4
I32 = mybir.dt.int32
U8 = mybir.dt.uint8
AF = mybir.ActivationFunctionType
ALU = mybir.AluOpType
AXX = mybir.AxisListType.X

SHIFT = 12.0
FP8_MAX = 448.0
HALF_SMALL = 2.0 ** -5


def build_nc(S=4096, SQ=2048, D=1024, QB=256, dbg=False):
    NT = S // 128
    TPW = 4
    NW = NT // TPW
    NQB = SQ // QB
    NSUB = QB // 128
    KC = S // 128
    OC = D // 128
    OQ = D // 256
    inv_d = 1.0 / D
    NCONST = 7 * 16
    DLR = 16 * 8 if dbg else 0
    NPP1 = NW * 16
    NTR1 = NT

    nc = bass.Bass("TRN2", target_bir_lowering=False, detect_race_conditions=False)
    xb_h = nc.dram_tensor("xb", [S, D], F32, kind="ExternalInput").ap()
    xo_h = nc.dram_tensor("xo", [SQ, D], F32, kind="ExternalInput").ap()
    wqt_h = nc.dram_tensor("wqt", [128, OC, OC, 128], F32, kind="ExternalInput").ap()
    wkt_h = nc.dram_tensor("wkt", [128, OC, OC, 128], F32, kind="ExternalInput").ap()
    wvt_h = nc.dram_tensor("wvt", [128, OC, D], F32, kind="ExternalInput").ap()
    wot_h = nc.dram_tensor("wot", [128, OQ, OC, 256], F32, kind="ExternalInput").ap()
    wft_h = nc.dram_tensor("wft", [128, OQ, OC, 256], F32, kind="ExternalInput").ap()
    bq_h = nc.dram_tensor("bq", [D], F32, kind="ExternalInput").ap()
    bk_h = nc.dram_tensor("bk", [D], F32, kind="ExternalInput").ap()
    bv_h = nc.dram_tensor("bv", [D], F32, kind="ExternalInput").ap()
    bo_h = nc.dram_tensor("bo", [D], F32, kind="ExternalInput").ap()
    fb_h = nc.dram_tensor("fb", [D], F32, kind="ExternalInput").ap()
    sin_h = nc.dram_tensor("sin", [1], F32, kind="ExternalInput").ap()
    sw_h = nc.dram_tensor("sw", [1], F32, kind="ExternalInput").ap()
    y_h = nc.dram_tensor("y", [SQ, D], F32, kind="ExternalOutput").ap()
    wqb_h = nc.dram_tensor("wqb", [128, OC, OC, 128], BF16, kind="Internal").ap()
    wob_h = nc.dram_tensor("wob", [128, OQ, OC, 256], BF16, kind="Internal").ap()
    wfb_h = nc.dram_tensor("wfb", [128, OQ, OC, 256], BF16, kind="Internal").ap()
    if dbg:
        dbg_specs = [("d_msc", [128, KC]), ("d_lrow", [1, QB]), ("d_lcol", [128, 2]),
                     ("d_rl", [128, 2]), ("d_e0", [128, QB]), ("d_av0", [128, QB]),
                     ("d_y1", [128, D]), ("d_x2", [128, D]), ("d_k0", [128, 512]),
                     ("d_v0", [128, 512]), ("d_lf2", [128, QB]), ("d_xq", [128, D]),
                     ("d_t2", [128, D]), ("d_id", [128, 128]), ("d_q0", [128, QB]),
                     ("d_sqd", [128, D]), ("d_rt", [128, 2]), ("d_ext", [128, QB]),
                     ("d_wk", [128, D]), ("d_lntm", [128, D]), ("d_lf", [128, 512]),
                     ("d_rt1", [128, 2]), ("d_rec1", [128, 2]), ("d_mu1", [128, 2]),
                     ("d_sqd1", [128, D]), ("d_xt", [128, D])]
        dbg_h = {n: nc.dram_tensor(n, sh, F32, kind="ExternalOutput").ap() for n, sh in dbg_specs}


    import contextlib
    es = contextlib.ExitStack()
    with es:
        def sb(name, shape, dt=F32):
            return es.enter_context(nc.sbuf_tensor(name, shape, dt))

        def psum(name, shape, dt=F32):
            return es.enter_context(nc.psum_tensor(name, shape, dt))

        def sem(name):
            return es.enter_context(nc.semaphore(name))

        k_s = sb("k_s", [128, OC, S], BF16)
        v_s = sb("v_s", [128, KC, D], BF16)
        bo_bc = sb("bo_bc", [128, D], BF16)
        bq_t = sb("bq_t", [128, OC])
        bk_t = sb("bk_t", [128, OC])
        bv_row = sb("bv_row", [1, D], BF16)
        fb_div = sb("fb_div", [1, D], BF16)
        msc = sb("msc", [128, KC])
        ident_b = sb("ident_b", [128, 128], BF16)
        ones_row = sb("ones_row", [1, 128], BF16)
        ones_col = sb("ones_col", [128, 1], BF16)
        cst = sb("cst", [128, 4])
        sin_bc = sb("sin_bc", [128, 1])
        sw_bc = sb("sw_bc", [128, 1])
        stot_bc = sb("stot_bc", [128, 1])
        rsin_bc = sb("rsin_bc", [128, 1])
        rstot_bc = sb("rstot_bc", [128, 1])
        l_row = sb("l_row", [1, QB])
        l_col = sb("l_col", [128, NSUB])
        r_l = sb("r_l", [128, NSUB])
        musum = sb("musum", [128, 4])
        sqsum = sb("sqsum", [128, 4])
        mu_t = sb("mu_t", [128, 4])
        m2_t = sb("m2_t", [128, 4])
        var_t = sb("var_t", [128, 4])
        rec_t = sb("rec_t", [128, 4])
        r_t = sb("r_t", [128, 4])
        rp_t = sb("rp_t", [128, 4])
        bp_t = sb("bp_t", [128, 4])
        msq_t = sb("msq_t", [128, 4])
        fence_s = sb("fence_s", [1, 4], BF16)

        p_tr = psum("p_tr", [128, 8, 128], BF16)
        p_m = [psum("p_m0", [128, 512]), psum("p_m1", [128, 512])]
        p_s = [psum("p_s0", [128, 512]), psum("p_s1", [128, 512])]
        p_a = [psum("p_a0", [128, 512]), psum("p_a1", [128, 512])]
        p_l = psum("p_l", [1, QB])
        pool_kv = [p_m[0], p_m[1], p_s[0], p_s[1], p_a[0], p_a[1]]
        pool_q = [p_m[0], p_m[1], p_a[0], p_a[1]]
        pool_sc = [p_s[0], p_s[1], p_a[0], p_a[1], p_m[0], p_m[1]]
        pool_av = [p_a[0], p_a[1], p_s[0], p_s[1]]
        pool_pf = [p_m[0], p_m[1], p_s[0], p_s[1], p_a[0], p_a[1]]

        s_c = sem("s_c");   s_d0 = sem("s_d0"); s_a0 = sem("s_a0"); s_d0a = sem("s_d0a")
        s_w1 = sem("s_w1"); s_x = sem("s_x");   s_sq = sem("s_sq"); s_vr = sem("s_vr")
        s_r = sem("s_r");   s_ln = sem("s_ln"); s_trp = sem("s_trp"); s_tre = sem("s_tre")
        s_pp = sem("s_pp"); s_pe = sem("s_pe"); s_sp = sem("s_sp");  s_see = sem("s_see")
        s_sec = sem("s_sec"); s_lp = sem("s_lp"); s_le = sem("s_le"); s_lr = sem("s_lr")
        s_rc = sem("s_rc"); s_ap = sem("s_ap"); s_ae = sem("s_ae");  s_w2 = sem("s_w2")
        s_y1 = sem("s_y1")
        s_out = sem("s_out"); s_bar = sem("s_bar"); s_ci = sem("s_ci"); s_wk = sem("s_wk")
        s_bc = sem("s_bc"); s_pc = sem("s_pc"); s_w3 = sem("s_w3")

        block = es.enter_context(nc.Block())

        with contextlib.ExitStack() as p1:
            def sb1(name, shape, dt=F32):
                return p1.enter_context(nc.sbuf_tensor(name, shape, dt))

            wk_blk = sb1("wk_blk", [128, 4, OC, 128], BF16)
            wv_s = sb1("wv_s", [128, OC, D], BF16)
            x_t = sb1("x_t", [128, 4, D])
            ln_tm = sb1("ln_tm", [128, 3, D], BF16)
            lf = sb1("lf", [128, 2, OC, 512], BF16)
            io_kc = sb1("io_kc", [128, KC], I32)
            msf = sb1("msf", [128, KC])
            msz = sb1("msz", [128, KC])
            idrow = sb1("idrow", [128, 128])
            idcol = sb1("idcol", [128, 1])
            sqd = sb1("sqd", [128, 2, D])

            @block.gpsimd
            def _(g):
                g.dma_start(out=bo_bc[:, :], in_=bass.AP(bo_h.tensor, 0, [[0, 128], [1, D]])).then_inc(s_c, 16)
                g.dma_start(out=bq_t[:, :], in_=bass.AP(bq_h.tensor, 0, [[1, 128], [128, OC]]),
                            allow_slow_non_contiguous=True).then_inc(s_c, 16)
                g.dma_start(out=bk_t[:, :], in_=bass.AP(bk_h.tensor, 0, [[1, 128], [128, OC]]),
                            allow_slow_non_contiguous=True).then_inc(s_c, 16)
                g.dma_start(out=bv_row[:, :], in_=bass.AP(bv_h.tensor, 0, [[0, 1], [1, D]])).then_inc(s_c, 16)
                g.dma_start(out=sqd[0:1, 0, :], in_=bass.AP(fb_h.tensor, 0, [[0, 1], [1, D]])).then_inc(s_c, 16)
                g.dma_start(out=sin_bc[:, :], in_=bass.AP(sin_h.tensor, 0, [[0, 128], [1, 1]])).then_inc(s_c, 16)
                g.dma_start(out=sw_bc[:, :], in_=bass.AP(sw_h.tensor, 0, [[0, 128], [1, 1]])).then_inc(s_c, 16)
                g.iota(io_kc[:, :], [[128, KC]], channel_multiplier=1)
                g.iota(idrow[:, :], [[1, 128]], channel_multiplier=0, allow_small_or_imprecise_dtypes=True)
                g.iota(idcol[:, :], [[1, 1]], channel_multiplier=1, allow_small_or_imprecise_dtypes=True)
                g.drain().then_inc(s_ci, 1)
                g.dma_start(out=wv_s[:, 0:4, :], in_=wvt_h[:, 0:4, :]).then_inc(s_w1, 16)
                g.dma_start(out=wv_s[:, 4:8, :], in_=wvt_h[:, 4:8, :]).then_inc(s_w1, 16)
                pcast = [(dst, srcw, c) for dst, srcw in
                         ((wqb_h, wqt_h), (wob_h, wot_h), (wfb_h, wft_h))
                         for c in range(8)]
                for w in range(NW):
                    for oc in range(OC):
                        bi = w * 8 + oc
                        if bi >= 4:
                            bp2 = bi - 4
                            g.wait_ge(s_pp, (bp2 // 8) * 16 + (bp2 % 8) + 1)
                        g.dma_start(out=wk_blk[:, bi % 4, :, :],
                                    in_=wkt_h[:, oc, :, :]).then_inc(s_wk, 16)
                        if bi % 2 == 1 and bi // 2 < len(pcast):
                            dst, srcw, c = pcast[bi // 2]
                            g.dma_start(out=dst[c * 16:(c + 1) * 16, :, :, :],
                                        in_=srcw[c * 16:(c + 1) * 16, :, :, :]).then_inc(s_pc, 16)
                for j in range(NW * 4, len(pcast)):
                    dst, srcw, c = pcast[j]
                    g.dma_start(out=dst[c * 16:(c + 1) * 16, :, :, :],
                                in_=srcw[c * 16:(c + 1) * 16, :, :, :]).then_inc(s_pc, 16)
                g.dma_start(out=fence_s[0:1, 0:1], in_=wqb_h[127:128, OC - 1, OC - 1, 127:128]).then_inc(s_pc, 16)
                g.dma_start(out=fence_s[0:1, 1:2], in_=wob_h[127:128, OQ - 1, OC - 1, 255:256]).then_inc(s_pc, 16)
                g.dma_start(out=fence_s[0:1, 2:3], in_=wfb_h[127:128, OQ - 1, OC - 1, 255:256]).then_inc(s_pc, 16)
                g.wait_ge(s_pc, 16 * (len(pcast) + 3))
                g.wait_ge(s_c, NCONST)
                g.wait_ge(s_w1, 32)
                if dbg:
                    g.wait_ge(s_pe, 16)
                    for h, t in [(dbg_h["d_wk"], wv_s[:, 0, :]), (dbg_h["d_lntm"], ln_tm[:, 0, :]),
                                 (dbg_h["d_lf"], lf[:, 0, :]), (dbg_h["d_rt1"], r_t[:, :]),
                                 (dbg_h["d_rec1"], rec_t[:, :]), (dbg_h["d_mu1"], mu_t[:, :]),
                                 (dbg_h["d_sqd1"], sqd[:, :]), (dbg_h["d_xt"], x_t[:, 0, :])]:
                        g.dma_start(out=h[:, :], in_=t).then_inc(s_lr, 16)
                    g.wait_ge(s_lr, 16 * 8)
                g.nop().then_inc(s_bar, 1)

            @block.vector
            def _(v):
                v.wait_ge(s_c, NCONST)
                v.wait_ge(s_ci, 1)
                v.memset(ones_row[:, :], 1.0)
                v.memset(ones_col[:, :], 1.0)
                v.memset(cst[:, 0:1], -SHIFT)
                v.memset(cst[:, 1:2], 4.0 / S)
                v.memset(cst[:, 2:3], -2.0)
                v.memset(cst[:, 3:4], -0.5)
                v.tensor_copy(msf[:, :], io_kc[:, :])
                v.tensor_scalar(ident_b[:, :], idrow[:, :], idcol[:, :1], None, ALU.is_equal)
                v.tensor_scalar(stot_bc[:, :], sin_bc[:, :], sw_bc[:, :1], None, ALU.mult)
                v.reciprocal(rsin_bc[:, :], sin_bc[:, :])
                v.drain()
                v.reciprocal(rstot_bc[:, :], stot_bc[:, :])
                v.drain()
                v.tensor_scalar(fb_div[0:1, :], sqd[0:1, 0, :], rstot_bc[0:1, 0:1], None, ALU.mult)
                v.drain().then_inc(s_d0a, 1)
                v.wait_ge(s_a0, 1)
                v.tensor_scalar(msc[:, :], msz[:, :], float(D) ** -0.5, None, ALU.mult)
                v.drain().then_inc(s_d0, 1)
                def kv_evac(v, g):
                    w, j = g // 16, g % 16
                    v.wait_ge(s_pp, g + 1)
                    if j < 8:
                        v.tensor_scalar(k_s[:, j, w * 512:(w + 1) * 512], pool_kv[g % 6][:, :],
                                        bk_t[:, j:j + 1], None, ALU.add).then_inc(s_pe, 1)
                    else:
                        tc, oh = (j - 8) // 2, (j - 8) % 2
                        v.tensor_copy(v_s[:, w * TPW + tc, oh * 512:(oh + 1) * 512],
                                      pool_kv[g % 6][:, :]).then_inc(s_pe, 1)

                def lf_evac(v, T):
                    w3 = T // 4
                    v.wait_ge(s_trp, T + 1)
                    if w3 >= 1:
                        v.wait_ge(s_pp, 16 * w3)
                    v.tensor_copy(lf[:, w3 % 2, :, (T % 4) * 128:(T % 4 + 1) * 128],
                                  p_tr[:, :, :]).then_inc(s_tre, 1)

                for w in range(NW):
                    for t in range(TPW):
                        T = w * TPW + t
                        par = T % 4
                        sp = T % 2
                        v.wait_ge(s_x, 16 * (T + 1))
                        v.reduce_sum(musum[:, par:par + 1], x_t[:, T % 4, :], axis=AXX)
                        v.wait_ge(s_sq, T + 1)
                        v.reduce_sum(sqsum[:, par:par + 1], sqd[:, sp, :], axis=AXX).then_inc(s_vr, 1)
                        if par % 2 == 1:
                            sl = par - 1
                            if T >= 4:
                                v.wait_ge(s_r, T // 2 - 1)
                            v.drain()
                            v.tensor_scalar(mu_t[:, sl:sl + 2], musum[:, sl:sl + 2], inv_d, None, ALU.mult)
                            v.drain()
                            v.tensor_scalar(m2_t[:, sl:sl + 2], sqsum[:, sl:sl + 2], inv_d, None, ALU.mult)
                            v.tensor_tensor(msq_t[:, sl:sl + 2], mu_t[:, sl:sl + 2], mu_t[:, sl:sl + 2], ALU.mult)
                            v.drain()
                            v.tensor_tensor(var_t[:, sl:sl + 2], m2_t[:, sl:sl + 2], msq_t[:, sl:sl + 2], ALU.subtract)
                            v.drain()
                            v.tensor_scalar(var_t[:, sl:sl + 2], var_t[:, sl:sl + 2], 1e-5, None, ALU.add)
                            v.drain()
                            v.reciprocal(rec_t[:, sl:sl + 2], var_t[:, sl:sl + 2])
                            v.drain().then_inc(s_bc, 1)
                    if w >= 1:
                        for j in range(16):
                            kv_evac(v, (w - 1) * 16 + j)
                    for t in range(TPW):
                        T = w * TPW + t
                        par = T % 4
                        if par % 2 == 0:
                            v.wait_ge(s_r, T // 2 + 1)
                            v.tensor_tensor(bp_t[:, par:par + 2], mu_t[:, par:par + 2],
                                            r_t[:, par:par + 2], ALU.mult)
                            v.drain()
                            v.tensor_scalar(bp_t[:, par:par + 2], bp_t[:, par:par + 2], -1.0, None, ALU.mult)
                            v.drain()
                        if T >= 3:
                            v.wait_ge(s_trp, T - 2)
                        v.tensor_scalar(ln_tm[:, T % 3, :], x_t[:, T % 4, :], r_t[:, par:par + 1],
                                        bp_t[:, par:par + 1], ALU.mult, ALU.add).then_inc(s_ln, 1)
                        if t >= 1:
                            lf_evac(v, T - 1)
                    lf_evac(v, w * TPW + 3)
                for j in range(16):
                    kv_evac(v, (NW - 1) * 16 + j)
                v.drain()
                v.nop().then_inc(s_bar, 1)

            @block.scalar
            def _(sc):
                sc.wait_ge(s_d0a, 1)
                sc.activation(msz[:, :], msf[:, :], AF.Square, bias=cst[:, 2:3], scale=cst[:, 1:2])
                sc.drain()
                sc.activation(msz[:, :], msz[:, :], AF.Exp, scale=cst[:, 3:4])
                sc.drain().then_inc(s_a0, 1)
                for STEP in range(NT + 1):
                    if STEP >= 1 and (STEP - 1) % 2 == 1:
                        T = STEP - 1
                        sl = (T % 4) - 1
                        sc.wait_ge(s_bc, T // 2 + 1)
                        sc.activation(r_t[:, sl:sl + 2], rec_t[:, sl:sl + 2], AF.Sqrt)
                        sc.drain().then_inc(s_r, 1)
                    if STEP < NT:
                        T = STEP
                        sc.wait_ge(s_x, 16 * (T + 1))
                        if T >= 2:
                            sc.wait_ge(s_vr, T - 1)
                        sc.activation(sqd[:, T % 2, :], x_t[:, T % 4, :], AF.Square).then_inc(s_sq, 1)
                sc.drain()
                sc.nop().then_inc(s_bar, 1)

            @block.sync
            def _(sy):
                for T in range(NT):
                    xs = T % 4
                    if T >= 4:
                        sy.wait_ge(s_ln, T - 3)
                    sy.dma_start(out=x_t[:, xs, :], in_=xb_h[T * 128:(T + 1) * 128, :]).then_inc(s_x, 16)
                sy.wait_ge(s_x, 16 * NT)
                sy.nop().then_inc(s_bar, 1)

            @block.tensor
            def _(tn):
                tn.wait_ge(s_d0, 1)
                tn.wait_ge(s_w1, 32)
                for w in range(NW):
                    for t in range(TPW):
                        T = w * TPW + t
                        par = T % 4
                        tn.wait_ge(s_ln, T + 1)
                        if T >= 1:
                            tn.wait_ge(s_tre, T)
                        for dc in range(OC):
                            ins = tn.transpose(p_tr[:, dc, :], ln_tm[:, T % 3, dc * 128:(dc + 1) * 128],
                                               ident_b[:, :])
                        ins.then_inc(s_trp, 1)
                    tn.wait_ge(s_tre, TPW * (w + 1))
                    for oc in range(OC):
                        gk = w * 16 + oc
                        bi = w * 8 + oc
                        tn.wait_ge(s_wk, 16 * (bi + 1))
                        if gk >= 6:
                            tn.wait_ge(s_pe, gk - 5)
                        for dc in range(OC):
                            ins = tn.matmul(pool_kv[gk % 6][:, :], wk_blk[:, bi % 4, dc, :],
                                            lf[:, w % 2, dc, :], start=(dc == 0), stop=(dc == OC - 1))
                        ins.then_inc(s_pp, 1)
                    for tc in range(TPW):
                        gv0 = w * 16 + 8 + tc * 2
                        if gv0 >= 6:
                            tn.wait_ge(s_pe, gv0 - 4)
                        for dc in range(OC):
                            tn.matmul(pool_kv[gv0 % 6][:, :], lf[:, w % 2, dc, tc * 128:(tc + 1) * 128],
                                      wv_s[:, dc, 0:512], start=(dc == 0), stop=False)
                            tn.matmul(pool_kv[(gv0 + 1) % 6][:, :], lf[:, w % 2, dc, tc * 128:(tc + 1) * 128],
                                      wv_s[:, dc, 512:1024], start=(dc == 0), stop=False)
                        tn.matmul(pool_kv[gv0 % 6][:, :], ones_row[0:1, :], bv_row[0:1, 0:512],
                                  start=False, stop=True).then_inc(s_pp, 1)
                        tn.matmul(pool_kv[(gv0 + 1) % 6][:, :], ones_row[0:1, :], bv_row[0:1, 512:1024],
                                  start=False, stop=True).then_inc(s_pp, 1)
                tn.drain()
                tn.nop().then_inc(s_bar, 1)

        with contextlib.ExitStack() as p2:
            def sb2(name, shape, dt=F32):
                return p2.enter_context(nc.sbuf_tensor(name, shape, dt))

            e_s = sb2("e_s", [128, KC, QB], BF16)
            x2 = sb2("x2", [128, NSUB, D])
            q_s = sb2("q_s", [128, OC, QB], BF16)
            lf2 = sb2("lf2", [128, OC, QB], BF16)
            y1 = sb2("y1", [128, NSUB, D])
            y2 = sb2("y2", [128, 2, 256])
            t2 = sb2("t2", [128, 512])
            sqd2 = sb2("sqd2", [128, 2, 512])
            xq_tm = sb2("xq_tm", [128, D], BF16)
            ex_t = sb2("ex_t", [128, 3, QB])
            q8a = sb2("q8a", [128, 512], FP8)
            q8b = sb2("q8b", [128, 512], FP8)
            mB = sb2("mB", [128, 512], BF16)
            mask8 = sb2("mask8", [128, 512], U8)
            wq_blk = sb2("wq_blk", [128, 4, OC, 128], BF16)
            w2q = sb2("w2q", [128, 2, OC, 256], BF16)

            SEMC = {}
            def cg(s):
                return SEMC.get(id(s), 0)
            def ca(s, n):
                SEMC[id(s)] = cg(s) + n
                return SEMC[id(s)]
            for s_, v_ in [(s_pp, NPP1), (s_pe, NPP1), (s_trp, NTR1), (s_tre, NTR1),
                           (s_ln, NT), (s_sq, NT), (s_vr, NT), (s_r, NT // 2),
                           (s_x, 16 * NT)]:
                SEMC[id(s_)] = v_
            EV = {k: [] for k in ("g", "sy", "sc", "v", "tn")}
            def ev(eng, fn, waits=(), inc=None, n=1):
                EV[eng].append((fn, list(waits), inc, n))
                return ca(inc, n) if inc is not None else None

            def E_lnqT(qb, qs):
                def fn(tn, qs=qs):
                    for dc in range(OC):
                        ins = tn.transpose(p_tr[:, dc, :], xq_tm[:, dc * 128:(dc + 1) * 128], ident_b[:, :])
                    return ins
                return fn

            def E_tr_evac(dst, qs):
                def fn(v, dst=dst, qs=qs):
                    return v.tensor_copy(dst[:, :, qs * 128:(qs + 1) * 128], p_tr[:, :, :])
                return fn

            def E_qproj(qb, oc):
                def fn(tn, qb=qb, oc=oc):
                    bi = qb * OC + oc
                    for dc in range(OC):
                        ins = tn.matmul(pool_q[oc % 4][:, 0:QB], wq_blk[:, bi % 4, dc, :],
                                        lf2[:, dc, :], start=(dc == 0), stop=(dc == OC - 1))
                    return ins
                return fn

            def E_qevac(qb, oc):
                def fn(v, oc=oc):
                    return v.tensor_scalar(q_s[:, oc, :], pool_q[oc % 4][:, 0:QB],
                                           bq_t[:, oc:oc + 1], None, ALU.add)
                return fn

            def E_scores(qb, kc):
                def fn(tn, kc=kc):
                    for oc in range(OC):
                        ins = tn.matmul(pool_sc[kc % 6][:, 0:QB], k_s[:, oc, kc * 128:(kc + 1) * 128],
                                        q_s[:, oc, :], start=(oc == 0), stop=(oc == OC - 1))
                    return ins
                return fn

            def E_exp(qb, kc):
                def fn(sc, kc=kc):
                    return sc.activation(ex_t[:, kc % 3, :], pool_sc[kc % 6][:, 0:QB], AF.Exp,
                                         bias=cst[:, 0:1], scale=msc[:, kc:kc + 1])
                return fn

            def E_cast(qb, kc):
                def fn(v, kc=kc):
                    return v.tensor_copy(e_s[:, kc, :], ex_t[:, kc % 3, :])
                return fn

            def E_lrow(qb):
                def fn(tn):
                    for kc in range(KC):
                        ins = tn.matmul(p_l[0:1, :], ones_col[:, :], e_s[:, kc, :],
                                        start=(kc == 0), stop=(kc == KC - 1))
                    return ins
                return fn

            def E_av(qb, dc):
                def fn(tn, dc=dc):
                    for kc in range(KC):
                        ins = tn.matmul(pool_av[dc % 4][:, 0:QB], v_s[:, kc, dc * 128:(dc + 1) * 128],
                                        e_s[:, kc, :], start=(kc == 0), stop=(kc == KC - 1))
                    return ins
                return fn

            def E_avevac(qb, dc):
                def fn(v, dc=dc):
                    return v.tensor_copy(q_s[:, dc, :], pool_av[dc % 4][:, 0:QB])
                return fn

            def E_proj(qb, oq, qs):
                def fn(tn, qb=qb, oq=oq, qs=qs):
                    QI = qb * 8 + oq
                    for dc in range(OC):
                        ins = tn.matmul(pool_pf[(oq * 2 + qs) % 6][:, 0:256],
                                        q_s[:, dc, qs * 128:(qs + 1) * 128],
                                        w2q[:, QI % 2, dc, :], start=(dc == 0), stop=(dc == OC - 1))
                    return ins
                return fn

            def E_projstt(qb, oq, qs):
                def fn(v, oq=oq, qs=qs):
                    return v.scalar_tensor_tensor(y1[:, qs, oq * 256:(oq + 1) * 256],
                                                  pool_pf[(oq * 2 + qs) % 6][:, 0:256], r_l[:, qs:qs + 1],
                                                  y1[:, qs, oq * 256:(oq + 1) * 256], ALU.mult, ALU.add)
                return fn

            def E_xqT(qb, qs):
                def fn(tn):
                    for dc in range(OC):
                        ins = tn.transpose(p_tr[:, dc, :], xq_tm[:, dc * 128:(dc + 1) * 128], ident_b[:, :])
                    return ins
                return fn

            def E_ffn(qb, oq, qs):
                def fn(tn, qb=qb, oq=oq, qs=qs):
                    QI = qb * 8 + 4 + oq
                    for dc in range(OC):
                        tn.matmul(pool_pf[(oq * 2 + qs) % 6][:, 0:256], lf2[:, dc, qs * 128:(qs + 1) * 128],
                                  w2q[:, QI % 2, dc, :], start=(dc == 0), stop=False)
                    return tn.matmul(pool_pf[(oq * 2 + qs) % 6][:, 0:256], ones_row[0:1, :],
                                     fb_div[0:1, oq * 256:(oq + 1) * 256], start=False, stop=True)
                return fn

            def E_ffnstt(qb, oq, qs):
                def fn(v, oq=oq, qs=qs):
                    yg = oq * 2 + qs
                    return v.scalar_tensor_tensor(y2[:, yg % 2, :], pool_pf[(oq * 2 + qs) % 6][:, 0:256],
                                                  stot_bc[:, :1], y1[:, qs, oq * 256:(oq + 1) * 256],
                                                  ALU.mult, ALU.add)
                return fn

            def E_y2dma(qb, oq, qs):
                def fn(sy, qb=qb, oq=oq, qs=qs):
                    r0 = qb * QB + qs * 128
                    return sy.dma_start(out=y_h[r0:r0 + 128, oq * 256:(oq + 1) * 256],
                                        in_=y2[:, (oq * 2 + qs) % 2, :])
                return fn

            def E_sq(src_ap_fn, h):
                def fn(sc, h=h):
                    return sc.activation(sqd2[:, h, :], src_ap_fn()[:, h * 512:(h + 1) * 512], AF.Square)
                return fn

            def E_sqv(src_ap_fn, h):
                def fn(v, h=h):
                    s = src_ap_fn()[:, h * 512:(h + 1) * 512]
                    return v.tensor_tensor(sqd2[:, h, :], s, s, ALU.mult)
                return fn

            def E_red(src_ap_fn, slot):
                def fn(v, slot=slot):
                    v.reduce_sum(musum[:, slot:slot + 1], src_ap_fn(), axis=AXX)
                    v.reduce_sum(sqsum[:, slot:slot + 1], sqd2[:, 0, :], axis=AXX)
                    v.reduce_sum(msq_t[:, slot:slot + 1], sqd2[:, 1, :], axis=AXX)
                    v.drain()
                    v.tensor_scalar(sqsum[:, slot:slot + 1], sqsum[:, slot:slot + 1],
                                    msq_t[:, slot:slot + 1], None, ALU.add)
                    v.drain()
                    v.tensor_scalar(mu_t[:, slot:slot + 1], musum[:, slot:slot + 1], inv_d, None, ALU.mult)
                    v.drain()
                    v.tensor_scalar(m2_t[:, slot:slot + 1], sqsum[:, slot:slot + 1], inv_d, None, ALU.mult)
                    v.drain()
                    v.tensor_scalar(msq_t[:, slot:slot + 1], mu_t[:, slot:slot + 1],
                                    mu_t[:, slot:slot + 1], None, ALU.mult)
                    v.drain()
                    v.tensor_scalar(var_t[:, slot:slot + 1], m2_t[:, slot:slot + 1],
                                    msq_t[:, slot:slot + 1], 1e-5, ALU.subtract, ALU.add)
                    v.drain()
                    v.reciprocal(rec_t[:, slot:slot + 1], var_t[:, slot:slot + 1])
                    return v.drain()
                return fn

            def E_sqrt(slot):
                def fn(sc, slot=slot):
                    sc.activation(r_t[:, slot:slot + 1], rec_t[:, slot:slot + 1], AF.Sqrt)
                    return sc.drain()
                return fn

            def E_bp1(slot):
                def fn(v, slot=slot):
                    v.tensor_scalar(bp_t[:, slot:slot + 1], mu_t[:, slot:slot + 1],
                                    r_t[:, slot:slot + 1], -1.0, ALU.mult, ALU.mult)
                    return v.drain()
                return fn

            def E_norm1(qs):
                def fn(v, qs=qs):
                    return v.tensor_scalar(xq_tm[:, :], x2[:, qs, :], r_t[:, qs:qs + 1],
                                           bp_t[:, qs:qs + 1], ALU.mult, ALU.add)
                return fn

            def E_bp2(slot):
                def fn(v, slot=slot):
                    v.tensor_scalar(rp_t[:, slot:slot + 1], r_t[:, slot:slot + 1],
                                    rsin_bc[:, :1], None, ALU.mult)
                    v.drain()
                    v.tensor_scalar(bp_t[:, slot:slot + 1], mu_t[:, slot:slot + 1],
                                    rp_t[:, slot:slot + 1], -1.0, ALU.mult, ALU.mult)
                    return v.drain()
                return fn

            def E_norm2h(qs, h):
                def fn(v, qs=qs, h=h):
                    v.tensor_scalar(t2[:, :], y1[:, qs, h * 512:(h + 1) * 512], rp_t[:, qs:qs + 1],
                                    bp_t[:, qs:qs + 1], ALU.mult, ALU.add)
                    return v.tensor_scalar(t2[:, :], t2[:, :], FP8_MAX, -FP8_MAX, ALU.min, ALU.max)
                return fn

            def E_quant_h(qs, h, part):
                def fn(v, qs=qs, h=h, part=part):
                    xs = xq_tm[:, h * 512:(h + 1) * 512]
                    if part == 0:
                        v.tensor_copy(q8a[:, :], t2[:, :])
                        v.tensor_scalar(q8b[:, :], t2[:, :], 0.5, None, ALU.mult)
                        return v.tensor_copy(xs, q8a[:, :])
                    else:
                        v.tensor_scalar(mB[:, :], q8b[:, :], 2.0, None, ALU.mult)
                        v.tensor_scalar(mask8[:, :], t2[:, :], float(HALF_SMALL), None, ALU.is_ge)
                        v.copy_predicated(xs, mask8[:, :], mB[:, :])
                        v.tensor_scalar(mask8[:, :], t2[:, :], float(-HALF_SMALL), None, ALU.is_le)
                        return v.copy_predicated(xs, mask8[:, :], mB[:, :])
                return fn

            def E_pred(oq, qs):
                def fn(v, oq=oq, qs=qs):
                    pm = musum if qs == 0 else sqsum
                    ps = rp_t if qs == 0 else bp_t
                    sl = y1[:, qs, oq * 256:(oq + 1) * 256]
                    v.reduce_sum(pm[:, oq:oq + 1], sl, axis=AXX)
                    v.tensor_tensor(t2[:, 0:256], sl, sl, ALU.mult)
                    v.drain()
                    return v.reduce_sum(ps[:, oq:oq + 1], t2[:, 0:256], axis=AXX)
                return fn

            def E_redp(qs):
                def fn(v, qs=qs):
                    pm = musum if qs == 0 else sqsum
                    ps = rp_t if qs == 0 else bp_t
                    v.reduce_sum(var_t[:, qs:qs + 1], pm[:, 0:4], axis=AXX)
                    v.reduce_sum(rec_t[:, qs:qs + 1], ps[:, 0:4], axis=AXX)
                    v.drain()
                    v.tensor_scalar(mu_t[:, qs:qs + 1], var_t[:, qs:qs + 1], inv_d, None, ALU.mult)
                    v.drain()
                    v.tensor_scalar(m2_t[:, qs:qs + 1], rec_t[:, qs:qs + 1], inv_d, None, ALU.mult)
                    v.drain()
                    v.tensor_scalar(msq_t[:, qs:qs + 1], mu_t[:, qs:qs + 1],
                                    mu_t[:, qs:qs + 1], None, ALU.mult)
                    v.drain()
                    v.tensor_scalar(var_t[:, qs:qs + 1], m2_t[:, qs:qs + 1],
                                    msq_t[:, qs:qs + 1], 1e-5, ALU.subtract, ALU.add)
                    v.drain()
                    v.reciprocal(rec_t[:, qs:qs + 1], var_t[:, qs:qs + 1])
                    return v.drain()
                return fn

            def E_x2load(qb, qs):
                def fn(sy, qb=qb, qs=qs):
                    r0 = qb * QB + qs * 128
                    return sy.dma_start(out=x2[:, qs, :], in_=xo_h[r0:r0 + 128, :])
                return fn

            def E_lcopy(qb):
                def fn(sc):
                    sc.copy(l_row[0:1, :], p_l[0:1, :])
                    return sc.drain()
                return fn

            def E_redist(qb, ssub):
                def fn(g, ssub=ssub):
                    return g.dma_start(out=l_col[:, ssub:ssub + 1],
                                       in_=l_row[0:1, ssub * 128:(ssub + 1) * 128],
                                       allow_slow_non_contiguous=True)
                return fn

            def E_recip(qb):
                def fn(v):
                    v.reciprocal(r_l[:, :], l_col[:, :])
                    return v.drain()
                return fn

            def E_y1pre(qb, qs):
                def fn(v, qs=qs):
                    return v.tensor_add(y1[:, qs, :], x2[:, qs, :], bo_bc[:, :])
                return fn

            def E_wq(qb, oc):
                def fn(g, qb=qb, oc=oc):
                    bi = qb * OC + oc
                    return g.dma_start(out=wq_blk[:, bi % 4, :, :],
                                       in_=wqb_h[:, oc, :, :])
                return fn

            def E_w2(qb, j, which):
                def fn(g, qb=qb, j=j, which=which):
                    QI = qb * 8 + (0 if which == "o" else 4) + j
                    srcr = wob_h if which == "o" else wfb_h
                    return g.dma_start(out=w2q[:, QI % 2, :, :], in_=srcr[:, j, :, :])
                return fn

            qevac_done = {}
            lnq_done = {}
            quant_done = {}
            lnqT_done = {}
            xqT_done = {}
            cast_done = {}
            exp_done = {}
            scg = {}
            avp = {}
            ave_done = {}
            ppg = {}
            peg = {}
            wqc = {}
            w2c = {}
            y1_done = {}
            lr_c = {}
            le_c = {}
            sqc = {}
            rc = {}
            vrc = {}
            outc = {}

            def sched_LN1_sq(qb, qs):
                ev("v", E_sqv(lambda qb=qb, qs=qs: x2[:, qs, :], 0),
                   waits=[(s_x, 16 * NT + 16 * (qb * 2 + qs + 1))])
                ev("v", E_sqv(lambda qb=qb, qs=qs: x2[:, qs, :], 1))

            def sched_LN1_red(qb, qs):
                k = ("ln1", qb, qs)
                vrc[k] = ev("v", E_red(lambda qb=qb, qs=qs: x2[:, qs, :], qs), inc=s_vr)
                rc[k] = ev("sc", E_sqrt(qs), waits=[(s_vr, vrc[k])], inc=s_r)

            def sched_LN1_norm(qb, qs):
                k = ("ln1", qb, qs)
                ev("v", E_bp1(qs), waits=[(s_r, rc[k])])
                tw = cg(s_trp) + 1 if qs == 1 else cg(s_trp)
                lnq_done[(qb, qs)] = ev("v", E_norm1(qs), waits=[(s_trp, tw)], inc=s_ln)

            def sched_LN1(qb):
                for qs in range(NSUB):
                    sched_LN1_sq(qb, qs)
                    sched_LN1_red(qb, qs)
                    sched_LN1_norm(qb, qs)

            def sched_quant_block(qb, qs, part):
                k = ("ln2", qb, qs)
                if part == 0:
                    ev("v", E_sqv(lambda qb=qb, qs=qs: y1[:, qs, :], 0))
                    ev("v", E_sqv(lambda qb=qb, qs=qs: y1[:, qs, :], 1))
                elif part == 1:
                    vrc[k] = ev("v", E_red(lambda qb=qb, qs=qs: y1[:, qs, :], qs), inc=s_vr)
                    rc[k] = ev("sc", E_sqrt(qs), waits=[(s_vr, vrc[k])], inc=s_r)
                elif part == 2:
                    ev("v", E_bp2(qs), waits=[(s_r, rc[k])])
                    ev("v", E_norm2h(qs, 0))
                elif part == 3:
                    ev("v", E_quant_h(qs, 0, 0), waits=[(s_trp, cg(s_trp))])
                    ev("v", E_quant_h(qs, 0, 1))
                elif part == 4:
                    ev("v", E_norm2h(qs, 1))
                    ev("v", E_quant_h(qs, 1, 0))
                elif part == 5:
                    quant_done[(qb, qs)] = ev("v", E_quant_h(qs, 1, 1), inc=s_ln)

            for e in ("g", "sy", "sc", "v", "tn"):
                EV[e].append((None, [(s_bar, 5)], None, 1))
            for qs in range(NSUB):
                ev("sy", E_x2load(0, qs), inc=s_x, n=16)
            for oc in range(4):
                wqc[(0, oc)] = ev("g", E_wq(0, oc), inc=s_w2, n=16)
            sched_LN1(0)

            def emit_ffn(fqb, in_att, qb, skip=()):
                for oq in range(OQ):
                    for qs in range(NSUB):
                        if (oq, qs) in skip:
                            continue
                        fi = oq * 2 + qs
                        w = [(s_tre, cg(s_tre)), (s_w2, w2c[fqb * 8 + 4 + oq])]
                        if fi >= 6:
                            w.append((s_pe, peg[("f", fqb, 0, fi - 6)]))
                        elif in_att and 2 <= fi <= 3:
                            w.append((s_ae, peg[("av", qb, 4 + fi)]))
                        elif in_att and 4 <= fi <= 5:
                            w.append((s_ae, peg[("av", qb, fi)]))
                        elif not in_att:
                            pmap = {0: (3, 0), 1: (3, 1), 2: (1, 0), 3: (1, 1),
                                    4: (2, 0), 5: (2, 1)}
                            w.append((s_pe, peg[("pr", fqb) + pmap[fi]]))
                        ppg[("f", fqb, oq, qs)] = ev("tn", E_ffn(fqb, oq, qs), waits=w, inc=s_pp)
                        if qs == 1 and oq < 2:
                            w2c[fqb * 8 + 4 + oq + 2] = ev("g", E_w2(fqb, oq + 2, "f"),
                                                           waits=[(s_pp, ppg[("f", fqb, oq, 1)])],
                                                           inc=s_w2, n=16)
                        if in_att and qs == 1 and oq >= 2:
                            w2c[qb * 8 + (oq - 2)] = ev("g", E_w2(qb, oq - 2, "o"),
                                                        waits=[(s_pp, ppg[("f", fqb, oq, 1)])],
                                                        inc=s_w2, n=16)
                        wv_ = [(s_pp, ppg[("f", fqb, oq, qs)])]
                        og = outc.get((fqb, oq - 1, qs))
                        if og is not None:
                            wv_.append((s_out, og))
                        peg[("f", fqb, oq, qs)] = ev("v", E_ffnstt(fqb, oq, qs),
                                                     waits=wv_, inc=s_pe)
                        outc[(fqb, oq, qs)] = ev("sy", E_y2dma(fqb, oq, qs),
                                                 waits=[(s_pe, peg[("f", fqb, oq, qs)])],
                                                 inc=s_out, n=16)
                        if in_att and fi in (6, 7):
                            pqs = fi - 6
                            y1_done[(qb, pqs)] = ev("v", E_y1pre(qb, pqs), inc=s_y1)
                            if qb + 1 < NQB:
                                ev("sy", E_x2load(qb + 1, pqs),
                                   waits=[(s_y1, y1_done[(qb, pqs)])], inc=s_x, n=16)
                                if fi == 7:
                                    sched_LN1_sq(qb + 1, 0)
                                    sched_LN1_red(qb + 1, 0)

            for qb in range(NQB + 1):
                att = qb < NQB
                tail = qb >= 1
                if att:
                    for qs in range(NSUB):
                        lnqT_done[(qb, qs)] = ev("tn", E_lnqT(qb, qs),
                                                 waits=[(s_ln, lnq_done[(qb, qs)]), (s_tre, cg(s_tre))],
                                                 inc=s_trp)
                        ev("v", E_tr_evac(lf2, qs), waits=[(s_trp, lnqT_done[(qb, qs)]),
                                                           (s_pp, cg(s_pp))], inc=s_tre)
                    for oc in range(OC):
                        w = [(s_w2, wqc[(qb, oc)]), (s_tre, cg(s_tre))]
                        if oc >= 4:
                            w.append((s_pe, peg[("q", qb, oc - 4)]))
                        elif qb >= 1:
                            qmap = {0: (3, 0), 1: (3, 1), 2: (2, 0), 3: (2, 1)}
                            w.append((s_pe, peg[("pr", qb - 1) + qmap[oc]]))
                        ppg[("q", qb, oc)] = ev("tn", E_qproj(qb, oc), waits=w, inc=s_pp)
                        nb, no = divmod(qb * 8 + oc + 4, 8)
                        if nb < NQB:
                            wqc[(nb, no)] = ev("g", E_wq(nb, no),
                                               waits=[(s_pp, ppg[("q", qb, oc)])], inc=s_w2, n=16)
                        peg[("q", qb, oc)] = ev("v", E_qevac(qb, oc),
                                                waits=[(s_pp, ppg[("q", qb, oc)])], inc=s_pe)
                    qevac_done[qb] = cg(s_pe)
                    qsched = {}
                    if tail:
                        for pos, jj in [(2, 0), (8, 1), (15, 6), (22, 7)]:
                            pos = min(max(1, pos * KC // 32), KC - 1)
                            qsched.setdefault(pos, []).append(jj)
                    for kc in range(KC):
                        w = [(s_pe, qevac_done[qb])]
                        if kc >= 6:
                            w.append((s_see, exp_done[(qb, kc - 6)]))
                        if qb >= 1 and kc < 6:
                            prevkc = kc + 6 * ((KC - 1 - kc) // 6)
                            w.append((s_see, exp_done[(qb - 1, prevkc)]))
                        scg[(qb, kc)] = ev("tn", E_scores(qb, kc), waits=w, inc=s_sp)
                        we = [(s_sp, scg[(qb, kc)])]
                        if kc >= 3:
                            we.append((s_sec, cast_done[(qb, kc - 3)]))
                        elif qb >= 1:
                            prevkc = kc + 3 * ((KC - 1 - kc) // 3)
                            we.append((s_sec, cast_done[(qb - 1, prevkc)]))
                        exp_done[(qb, kc)] = ev("sc", E_exp(qb, kc), waits=we, inc=s_see)
                        wc = [(s_see, exp_done[(qb, kc)])]
                        if qb >= 1 and kc == 0:
                            wc.append((s_ap, avp[(qb - 1, 7)]))
                        cast_done[(qb, kc)] = ev("v", E_cast(qb, kc), waits=wc, inc=s_sec)
                        for jj in qsched.get(kc, []):
                            sched_quant_block(qb - 1, jj // 6, jj % 6)
                    ev("tn", E_lrow(qb), waits=[(s_sec, cast_done[(qb, KC - 1)]),
                                                (s_le, le_c.get(qb - 1, 0))], inc=s_lp)
                    le_c[qb] = ev("sc", E_lcopy(qb), waits=[(s_lp, cg(s_lp)),
                                                            (s_lr, lr_c.get(qb - 1, 0))], inc=s_le)
                    for ssub in range(NSUB):
                        lr_c[qb] = ev("g", E_redist(qb, ssub),
                                      waits=([(s_le, le_c[qb])] if ssub == 0 else []) +
                                            ([(s_rc, cg(s_rc))] if ssub == 0 else []),
                                      inc=s_lr, n=16)
                    for dc in range(OC):
                        w = [(s_sec, cast_done[(qb, KC - 1)])]
                        if dc >= 4:
                            w.append((s_ae, peg[("av", qb, dc - 4)]))
                        avp[(qb, dc)] = ev("tn", E_av(qb, dc), waits=w, inc=s_ap)
                        peg[("av", qb, dc)] = ev("v", E_avevac(qb, dc),
                                                 waits=[(s_ap, avp[(qb, dc)])], inc=s_ae)
                        if tail:
                            if dc <= 2:
                                sched_quant_block(qb - 1, 0, dc + 2)
                            elif dc == 3:
                                sched_quant_block(qb - 1, 0, 5)
                                sched_quant_block(qb - 1, 1, 2)
                            elif dc == 4:
                                xqT_done[(qb - 1, 0)] = ev("tn", E_xqT(qb - 1, 0),
                                                           waits=[(s_ln, quant_done[(qb - 1, 0)]),
                                                                  (s_tre, cg(s_tre))], inc=s_trp)
                                ev("v", E_tr_evac(lf2, 0), waits=[(s_trp, xqT_done[(qb - 1, 0)]),
                                                                  (s_pp, cg(s_pp))], inc=s_tre)
                                sched_quant_block(qb - 1, 1, 3)
                            elif dc == 5:
                                sched_quant_block(qb - 1, 1, 4)
                            elif dc == 6:
                                sched_quant_block(qb - 1, 1, 5)
                        if dc == 6:
                            ev("v", E_recip(qb), waits=[(s_lr, lr_c[qb])], inc=s_rc)
                    ave_done[qb] = cg(s_ae)
                    if tail:
                        xqT_done[(qb - 1, 1)] = ev("tn", E_xqT(qb - 1, 1),
                                                   waits=[(s_ln, quant_done[(qb - 1, 1)]),
                                                          (s_tre, cg(s_tre))], inc=s_trp)
                        ev("v", E_tr_evac(lf2, 1), waits=[(s_trp, xqT_done[(qb - 1, 1)]),
                                                          (s_pp, cg(s_pp))], inc=s_tre)
                        emit_ffn(qb - 1, True, qb)
                    else:
                        for qs in range(NSUB):
                            y1_done[(qb, qs)] = ev("v", E_y1pre(qb, qs), inc=s_y1)
                            if qb + 1 < NQB:
                                ev("sy", E_x2load(qb + 1, qs), waits=[(s_y1, y1_done[(qb, qs)])],
                                   inc=s_x, n=16)
                        if qb + 1 < NQB:
                            sched_LN1_sq(qb + 1, 0)
                            sched_LN1_red(qb + 1, 0)
                        for j in range(2):
                            w2c[qb * 8 + j] = ev("g", E_w2(qb, j, "o"), inc=s_w2, n=16)
                    for oq in range(OQ):
                        for qs in range(NSUB):
                            w = [(s_ae, ave_done[qb]), (s_w2, w2c[qb * 8 + oq])]
                            pi = oq * 2 + qs
                            if pi >= 6:
                                w.append((s_pe, peg[("pr", qb, 0, pi - 6)]))
                            elif qb >= 1:
                                fmap = {0: (3, 0), 1: (3, 1), 2: (1, 0), 3: (1, 1),
                                        4: (2, 0), 5: (2, 1)}
                                w.append((s_pe, peg[("f", qb - 1) + fmap[pi]]))
                            ppg[("pr", qb, oq, qs)] = ev("tn", E_proj(qb, oq, qs), waits=w, inc=s_pp)
                            peg[("pr", qb, oq, qs)] = ev("v", E_projstt(qb, oq, qs),
                                                         waits=[(s_pp, ppg[("pr", qb, oq, qs)])], inc=s_pe)
                            if qb + 1 < NQB:
                                if (oq, qs) == (0, 1):
                                    sched_LN1_sq(qb + 1, 1)
                                elif (oq, qs) == (1, 1):
                                    sched_LN1_red(qb + 1, 1)
                                elif (oq, qs) == (2, 1):
                                    sched_LN1_norm(qb + 1, 0)
                            else:
                                ev("v", E_pred(oq, qs))
                        if oq < 2:
                            w2c[qb * 8 + oq + 2] = ev("g", E_w2(qb, oq + 2, "o"),
                                                      waits=[(s_pp, ppg[("pr", qb, oq, 1)])],
                                                      inc=s_w2, n=16)
                        else:
                            w2c[qb * 8 + 4 + (oq - 2)] = ev("g", E_w2(qb, oq - 2, "f"),
                                                            waits=[(s_pp, ppg[("pr", qb, oq, 1)])],
                                                            inc=s_w2, n=16)
                    if qb + 1 < NQB:
                        sched_LN1_norm(qb + 1, 1)
                else:
                    k0 = ("ln2", qb - 1, 0)
                    k1 = ("ln2", qb - 1, 1)
                    vrc[k0] = ev("v", E_redp(0), inc=s_vr)
                    vrc[k1] = ev("v", E_redp(1), inc=s_vr)
                    rc[k0] = ev("sc", E_sqrt(0), waits=[(s_vr, vrc[k0])], inc=s_r)
                    rc[k1] = ev("sc", E_sqrt(1), waits=[(s_vr, vrc[k1])], inc=s_r)
                    for jj in range(2, 6):
                        sched_quant_block(qb - 1, 0, jj)
                    xqT_done[(qb - 1, 0)] = ev("tn", E_xqT(qb - 1, 0),
                                               waits=[(s_ln, quant_done[(qb - 1, 0)]),
                                                      (s_tre, cg(s_tre))], inc=s_trp)
                    ev("v", E_tr_evac(lf2, 0), waits=[(s_trp, xqT_done[(qb - 1, 0)]),
                                                      (s_pp, cg(s_pp))], inc=s_tre)
                    for oq_e in (0, 1):
                        w_e = [(s_tre, cg(s_tre)), (s_w2, w2c[(qb - 1) * 8 + 4 + oq_e])]
                        pmap_e = {0: (3, 0), 2: (1, 0)}
                        w_e.append((s_pe, peg[("pr", qb - 1) + pmap_e[2 * oq_e]]))
                        ppg[("f", qb - 1, oq_e, 0)] = ev("tn", E_ffn(qb - 1, oq_e, 0),
                                                        waits=w_e, inc=s_pp)
                    for jj in range(2, 6):
                        sched_quant_block(qb - 1, 1, jj)
                    for oq_e in (0, 1):
                        wv_e = [(s_pp, ppg[("f", qb - 1, oq_e, 0)])]
                        if oq_e == 1:
                            wv_e.append((s_out, outc[(qb - 1, 0, 0)]))
                        peg[("f", qb - 1, oq_e, 0)] = ev("v", E_ffnstt(qb - 1, oq_e, 0),
                                                         waits=wv_e, inc=s_pe)
                        outc[(qb - 1, oq_e, 0)] = ev("sy", E_y2dma(qb - 1, oq_e, 0),
                                                     waits=[(s_pe, peg[("f", qb - 1, oq_e, 0)])],
                                                     inc=s_out, n=16)
                    xqT_done[(qb - 1, 1)] = ev("tn", E_xqT(qb - 1, 1),
                                               waits=[(s_ln, quant_done[(qb - 1, 1)]),
                                                      (s_tre, cg(s_tre))], inc=s_trp)
                    ev("v", E_tr_evac(lf2, 1), waits=[(s_trp, xqT_done[(qb - 1, 1)]),
                                                      (s_pp, cg(s_pp))], inc=s_tre)
                    emit_ffn(qb - 1, False, qb, skip=((0, 0), (1, 0)))
            EV["sy"].append((None, [(s_out, 16 * 8 * NQB)], None, 1))

            def run_events(e, key):
                for fn, waits, inc, n in EV[key]:
                    for s_, val in waits:
                        if val > 0:
                            e.wait_ge(s_, val)
                    if fn is not None:
                        ins = fn(e)
                        if inc is not None:
                            ins.then_inc(inc, n)

            @block.gpsimd
            def _(g):
                run_events(g, "g")

            @block.sync
            def _(sy):
                run_events(sy, "sy")

            @block.scalar
            def _(sc):
                run_events(sc, "sc")

            @block.vector
            def _(v):
                run_events(v, "v")

            @block.tensor
            def _(tn):
                run_events(tn, "tn")

    return nc


def kernel(L0_attn_x, L0_attn_Wq, L0_attn_bq, L0_attn_Wk, L0_attn_bk,
           L0_attn_Wv, L0_attn_bv, L0_attn_Wo, L0_attn_bo,
           L0_ffn_weight_fp8, L0_ffn_bias, L0_ffn_input_scale, L0_ffn_weight_scale):
    x = np.ascontiguousarray(np.asarray(L0_attn_x, dtype=np.float32))
    B, S, D = x.shape
    SQ = S // 2
    OC = D // 128
    OQ = D // 256

    def blk4(w, cols):
        nb = D // cols
        return np.ascontiguousarray(
            np.asarray(w, np.float32).T.reshape(OC, 128, nb, cols).transpose(1, 2, 0, 3))

    wqt = blk4(L0_attn_Wq, 128)
    wkt = blk4(L0_attn_Wk, 128)
    wvt = np.ascontiguousarray(
        np.asarray(L0_attn_Wv, np.float32).T.reshape(OC, 128, D).transpose(1, 0, 2))
    wot = blk4(L0_attn_Wo, 256)
    wft = blk4(L0_ffn_weight_fp8, 256)
    bq = np.ascontiguousarray(np.asarray(L0_attn_bq, np.float32))
    bk = np.ascontiguousarray(np.asarray(L0_attn_bk, np.float32))
    bv = np.ascontiguousarray(np.asarray(L0_attn_bv, np.float32))
    bo = np.ascontiguousarray(np.asarray(L0_attn_bo, np.float32))
    fb = np.ascontiguousarray(np.asarray(L0_ffn_bias, np.float32))
    sin = np.ascontiguousarray(np.asarray(L0_ffn_input_scale, np.float32))
    sw = np.ascontiguousarray(np.asarray(L0_ffn_weight_scale, np.float32))

    nc = build_nc(S=S, SQ=SQ, D=D, QB=256)
    in_maps = []
    for c in range(8):
        b, h = c // 2, c % 2
        in_maps.append({
            "xb": np.ascontiguousarray(x[b]),
            "xo": np.ascontiguousarray(x[b, h * SQ:(h + 1) * SQ]),
            "wqt": wqt, "wkt": wkt, "wvt": wvt, "wot": wot, "wft": wft,
            "bq": bq, "bk": bk, "bv": bv, "bo": bo, "fb": fb,
            "sin": sin, "sw": sw,
        })
    res = run_bass_kernel_spmd(nc, in_maps, core_ids=list(range(8)))
    out = np.empty((B, S, D), dtype=np.float32)
    for c in range(8):
        b, h = c // 2, c % 2
        out[b, h * SQ:(h + 1) * SQ] = res.results[c]["y"]
    return out


# version 41
# speedup vs baseline: 1.0727x; 1.0656x over previous
import numpy as np
import concourse.bass as bass
import concourse.mybir as mybir
from concourse.bass_utils import run_bass_kernel_spmd

F32 = mybir.dt.float32
BF16 = mybir.dt.bfloat16
FP8 = mybir.dt.float8e4
I32 = mybir.dt.int32
U8 = mybir.dt.uint8
AF = mybir.ActivationFunctionType
ALU = mybir.AluOpType
AXX = mybir.AxisListType.X

SHIFT = 12.0
FP8_MAX = 448.0
HALF_SMALL = 2.0 ** -5


def build_nc(S=4096, SQ=2048, D=1024, QB=256, dbg=False):
    NT = S // 128
    TPW = 4
    NW = NT // TPW
    NQB = SQ // QB
    NSUB = QB // 128
    KC = S // 128
    OC = D // 128
    OQ = D // 256
    inv_d = 1.0 / D
    NCONST = 7 * 16
    DLR = 16 * 8 if dbg else 0
    NPP1 = NW * 16
    NTR1 = NT

    nc = bass.Bass("TRN2", target_bir_lowering=False, detect_race_conditions=False)
    xb_h = nc.dram_tensor("xb", [S, D], F32, kind="ExternalInput").ap()
    xo_h = nc.dram_tensor("xo", [SQ, D], F32, kind="ExternalInput").ap()
    wqt_h = nc.dram_tensor("wqt", [128, OC, OC, 128], F32, kind="ExternalInput").ap()
    wkt_h = nc.dram_tensor("wkt", [128, OC, OC, 128], F32, kind="ExternalInput").ap()
    wvt_h = nc.dram_tensor("wvt", [128, OC, D], F32, kind="ExternalInput").ap()
    wot_h = nc.dram_tensor("wot", [128, OQ, OC, 256], F32, kind="ExternalInput").ap()
    wft_h = nc.dram_tensor("wft", [128, OQ, OC, 256], F32, kind="ExternalInput").ap()
    bq_h = nc.dram_tensor("bq", [D], F32, kind="ExternalInput").ap()
    bk_h = nc.dram_tensor("bk", [D], F32, kind="ExternalInput").ap()
    bv_h = nc.dram_tensor("bv", [D], F32, kind="ExternalInput").ap()
    bo_h = nc.dram_tensor("bo", [D], F32, kind="ExternalInput").ap()
    fb_h = nc.dram_tensor("fb", [D], F32, kind="ExternalInput").ap()
    sin_h = nc.dram_tensor("sin", [1], F32, kind="ExternalInput").ap()
    sw_h = nc.dram_tensor("sw", [1], F32, kind="ExternalInput").ap()
    y_h = nc.dram_tensor("y", [SQ, D], F32, kind="ExternalOutput").ap()
    wqb_h = nc.dram_tensor("wqb", [128, OC, OC, 128], BF16, kind="Internal").ap()
    wob_h = nc.dram_tensor("wob", [128, OQ, OC, 256], BF16, kind="Internal").ap()
    wfb_h = nc.dram_tensor("wfb", [128, OQ, OC, 256], BF16, kind="Internal").ap()
    if dbg:
        dbg_specs = [("d_msc", [128, KC]), ("d_lrow", [1, QB]), ("d_lcol", [128, 2]),
                     ("d_rl", [128, 2]), ("d_e0", [128, QB]), ("d_av0", [128, QB]),
                     ("d_y1", [128, D]), ("d_x2", [128, D]), ("d_k0", [128, 512]),
                     ("d_v0", [128, 512]), ("d_lf2", [128, QB]), ("d_xq", [128, D]),
                     ("d_t2", [128, D]), ("d_id", [128, 128]), ("d_q0", [128, QB]),
                     ("d_sqd", [128, D]), ("d_rt", [128, 2]), ("d_ext", [128, QB]),
                     ("d_wk", [128, D]), ("d_lntm", [128, D]), ("d_lf", [128, 512]),
                     ("d_rt1", [128, 2]), ("d_rec1", [128, 2]), ("d_mu1", [128, 2]),
                     ("d_sqd1", [128, D]), ("d_xt", [128, D])]
        dbg_h = {n: nc.dram_tensor(n, sh, F32, kind="ExternalOutput").ap() for n, sh in dbg_specs}


    import contextlib
    es = contextlib.ExitStack()
    with es:
        def sb(name, shape, dt=F32):
            return es.enter_context(nc.sbuf_tensor(name, shape, dt))

        def psum(name, shape, dt=F32):
            return es.enter_context(nc.psum_tensor(name, shape, dt))

        def sem(name):
            return es.enter_context(nc.semaphore(name))

        k_s = sb("k_s", [128, OC, S], BF16)
        v_s = sb("v_s", [128, KC, D], BF16)
        bo_bc = sb("bo_bc", [128, D], BF16)
        bq_t = sb("bq_t", [128, OC])
        bk_t = sb("bk_t", [128, OC])
        bv_row = sb("bv_row", [1, D], BF16)
        fb_div = sb("fb_div", [1, D], BF16)
        msc = sb("msc", [128, KC])
        ident_b = sb("ident_b", [128, 128], BF16)
        ones_row = sb("ones_row", [1, 128], BF16)
        ones_col = sb("ones_col", [128, 1], BF16)
        cst = sb("cst", [128, 4])
        sin_bc = sb("sin_bc", [128, 1])
        sw_bc = sb("sw_bc", [128, 1])
        stot_bc = sb("stot_bc", [128, 1])
        rsin_bc = sb("rsin_bc", [128, 1])
        rstot_bc = sb("rstot_bc", [128, 1])
        l_row = sb("l_row", [1, QB])
        l_col = sb("l_col", [128, NSUB])
        r_l = sb("r_l", [128, NSUB])
        musum = sb("musum", [128, 4])
        sqsum = sb("sqsum", [128, 4])
        mu_t = sb("mu_t", [128, 4])
        m2_t = sb("m2_t", [128, 4])
        var_t = sb("var_t", [128, 4])
        rec_t = sb("rec_t", [128, 4])
        r_t = sb("r_t", [128, 4])
        rp_t = sb("rp_t", [128, 4])
        bp_t = sb("bp_t", [128, 4])
        msq_t = sb("msq_t", [128, 4])
        fence_s = sb("fence_s", [1, 4], BF16)

        p_tr = psum("p_tr", [128, 8, 128], BF16)
        p_m = [psum("p_m0", [128, 512]), psum("p_m1", [128, 512])]
        p_s = [psum("p_s0", [128, 512]), psum("p_s1", [128, 512])]
        p_a = [psum("p_a0", [128, 512]), psum("p_a1", [128, 512])]
        p_l = psum("p_l", [1, QB])
        pool_kv = [p_m[0], p_m[1], p_s[0], p_s[1], p_a[0], p_a[1]]
        pool_q = [p_m[0], p_m[1], p_a[0], p_a[1]]
        pool_sc = [p_s[0], p_s[1], p_a[0], p_a[1], p_m[0], p_m[1]]
        pool_av = [p_a[0], p_a[1], p_s[0], p_s[1]]
        pool_pf = [p_m[0], p_m[1], p_s[0], p_s[1], p_a[0], p_a[1]]

        s_c = sem("s_c");   s_d0 = sem("s_d0"); s_a0 = sem("s_a0"); s_d0a = sem("s_d0a")
        s_w1 = sem("s_w1"); s_x = sem("s_x");   s_sq = sem("s_sq"); s_vr = sem("s_vr")
        s_r = sem("s_r");   s_ln = sem("s_ln"); s_trp = sem("s_trp"); s_tre = sem("s_tre")
        s_pp = sem("s_pp"); s_pe = sem("s_pe"); s_sp = sem("s_sp");  s_see = sem("s_see")
        s_sec = sem("s_sec"); s_lp = sem("s_lp"); s_le = sem("s_le"); s_lr = sem("s_lr")
        s_rc = sem("s_rc"); s_ap = sem("s_ap"); s_ae = sem("s_ae");  s_w2 = sem("s_w2")
        s_y1 = sem("s_y1")
        s_out = sem("s_out"); s_bar = sem("s_bar"); s_ci = sem("s_ci"); s_wk = sem("s_wk")
        s_bc = sem("s_bc"); s_pc = sem("s_pc"); s_w3 = sem("s_w3")

        block = es.enter_context(nc.Block())

        with contextlib.ExitStack() as p1:
            def sb1(name, shape, dt=F32):
                return p1.enter_context(nc.sbuf_tensor(name, shape, dt))

            wk_blk = sb1("wk_blk", [128, 4, OC, 128], BF16)
            wv_s = sb1("wv_s", [128, OC, D], BF16)
            x_t = sb1("x_t", [128, 4, D])
            ln_tm = sb1("ln_tm", [128, 3, D], BF16)
            lf = sb1("lf", [128, 2, OC, 512], BF16)
            io_kc = sb1("io_kc", [128, KC], I32)
            msf = sb1("msf", [128, KC])
            msz = sb1("msz", [128, KC])
            idrow = sb1("idrow", [128, 128])
            idcol = sb1("idcol", [128, 1])
            sqd = sb1("sqd", [128, 2, D])

            @block.gpsimd
            def _(g):
                g.dma_start(out=bo_bc[:, :], in_=bass.AP(bo_h.tensor, 0, [[0, 128], [1, D]])).then_inc(s_c, 16)
                g.dma_start(out=bq_t[:, :], in_=bass.AP(bq_h.tensor, 0, [[1, 128], [128, OC]]),
                            allow_slow_non_contiguous=True).then_inc(s_c, 16)
                g.dma_start(out=bk_t[:, :], in_=bass.AP(bk_h.tensor, 0, [[1, 128], [128, OC]]),
                            allow_slow_non_contiguous=True).then_inc(s_c, 16)
                g.dma_start(out=bv_row[:, :], in_=bass.AP(bv_h.tensor, 0, [[0, 1], [1, D]])).then_inc(s_c, 16)
                g.dma_start(out=sqd[0:1, 0, :], in_=bass.AP(fb_h.tensor, 0, [[0, 1], [1, D]])).then_inc(s_c, 16)
                g.dma_start(out=sin_bc[:, :], in_=bass.AP(sin_h.tensor, 0, [[0, 128], [1, 1]])).then_inc(s_c, 16)
                g.dma_start(out=sw_bc[:, :], in_=bass.AP(sw_h.tensor, 0, [[0, 128], [1, 1]])).then_inc(s_c, 16)
                g.iota(io_kc[:, :], [[128, KC]], channel_multiplier=1)
                g.iota(idrow[:, :], [[1, 128]], channel_multiplier=0, allow_small_or_imprecise_dtypes=True)
                g.iota(idcol[:, :], [[1, 1]], channel_multiplier=1, allow_small_or_imprecise_dtypes=True)
                g.drain().then_inc(s_ci, 1)
                g.dma_start(out=wv_s[:, 0:4, :], in_=wvt_h[:, 0:4, :]).then_inc(s_w1, 16)
                g.dma_start(out=wv_s[:, 4:8, :], in_=wvt_h[:, 4:8, :]).then_inc(s_w1, 16)
                pcast = [(dst, srcw, c) for dst, srcw in
                         ((wqb_h, wqt_h), (wob_h, wot_h), (wfb_h, wft_h))
                         for c in range(8)]
                for w in range(NW):
                    for oc in range(OC):
                        bi = w * 8 + oc
                        if bi >= 4:
                            bp2 = bi - 4
                            g.wait_ge(s_pp, (bp2 // 8) * 16 + (bp2 % 8) + 1)
                        g.dma_start(out=wk_blk[:, bi % 4, :, :],
                                    in_=wkt_h[:, oc, :, :]).then_inc(s_wk, 16)
                        if bi % 2 == 1 and bi // 2 < len(pcast):
                            dst, srcw, c = pcast[bi // 2]
                            g.dma_start(out=dst[c * 16:(c + 1) * 16, :, :, :],
                                        in_=srcw[c * 16:(c + 1) * 16, :, :, :]).then_inc(s_pc, 16)
                for j in range(NW * 4, len(pcast)):
                    dst, srcw, c = pcast[j]
                    g.dma_start(out=dst[c * 16:(c + 1) * 16, :, :, :],
                                in_=srcw[c * 16:(c + 1) * 16, :, :, :]).then_inc(s_pc, 16)
                g.dma_start(out=fence_s[0:1, 0:1], in_=wqb_h[127:128, OC - 1, OC - 1, 127:128]).then_inc(s_pc, 16)
                g.dma_start(out=fence_s[0:1, 1:2], in_=wob_h[127:128, OQ - 1, OC - 1, 255:256]).then_inc(s_pc, 16)
                g.dma_start(out=fence_s[0:1, 2:3], in_=wfb_h[127:128, OQ - 1, OC - 1, 255:256]).then_inc(s_pc, 16)
                g.wait_ge(s_pc, 16 * (len(pcast) + 3))
                g.wait_ge(s_c, NCONST)
                g.wait_ge(s_w1, 32)
                if dbg:
                    g.wait_ge(s_pe, 16)
                    for h, t in [(dbg_h["d_wk"], wv_s[:, 0, :]), (dbg_h["d_lntm"], ln_tm[:, 0, :]),
                                 (dbg_h["d_lf"], lf[:, 0, :]), (dbg_h["d_rt1"], r_t[:, :]),
                                 (dbg_h["d_rec1"], rec_t[:, :]), (dbg_h["d_mu1"], mu_t[:, :]),
                                 (dbg_h["d_sqd1"], sqd[:, :]), (dbg_h["d_xt"], x_t[:, 0, :])]:
                        g.dma_start(out=h[:, :], in_=t).then_inc(s_lr, 16)
                    g.wait_ge(s_lr, 16 * 8)
                g.nop().then_inc(s_bar, 1)

            @block.vector
            def _(v):
                v.wait_ge(s_c, NCONST)
                v.wait_ge(s_ci, 1)
                v.memset(ones_row[:, :], 1.0)
                v.memset(ones_col[:, :], 1.0)
                v.memset(cst[:, 0:1], -SHIFT)
                v.memset(cst[:, 1:2], 4.0 / S)
                v.memset(cst[:, 2:3], -2.0)
                v.memset(cst[:, 3:4], -0.5)
                v.tensor_copy(msf[:, :], io_kc[:, :])
                v.tensor_scalar(ident_b[:, :], idrow[:, :], idcol[:, :1], None, ALU.is_equal)
                v.tensor_scalar(stot_bc[:, :], sin_bc[:, :], sw_bc[:, :1], None, ALU.mult)
                v.reciprocal(rsin_bc[:, :], sin_bc[:, :])
                v.drain()
                v.reciprocal(rstot_bc[:, :], stot_bc[:, :])
                v.drain()
                v.tensor_scalar(fb_div[0:1, :], sqd[0:1, 0, :], rstot_bc[0:1, 0:1], None, ALU.mult)
                v.drain().then_inc(s_d0a, 1)
                v.wait_ge(s_a0, 1)
                v.tensor_scalar(msc[:, :], msz[:, :], float(D) ** -0.5, None, ALU.mult)
                v.drain().then_inc(s_d0, 1)
                def kv_evac(v, g):
                    w, j = g // 16, g % 16
                    v.wait_ge(s_pp, g + 1)
                    if j < 8:
                        v.tensor_scalar(k_s[:, j, w * 512:(w + 1) * 512], pool_kv[g % 6][:, :],
                                        bk_t[:, j:j + 1], None, ALU.add).then_inc(s_pe, 1)
                    else:
                        tc, oh = (j - 8) // 2, (j - 8) % 2
                        v.tensor_copy(v_s[:, w * TPW + tc, oh * 512:(oh + 1) * 512],
                                      pool_kv[g % 6][:, :]).then_inc(s_pe, 1)

                def lf_evac(v, T):
                    w3 = T // 4
                    v.wait_ge(s_trp, T + 1)
                    if w3 >= 1:
                        v.wait_ge(s_pp, 16 * w3)
                    v.tensor_copy(lf[:, w3 % 2, :, (T % 4) * 128:(T % 4 + 1) * 128],
                                  p_tr[:, :, :]).then_inc(s_tre, 1)

                for w in range(NW):
                    for t in range(TPW):
                        T = w * TPW + t
                        par = T % 4
                        sp = T % 2
                        v.wait_ge(s_x, 16 * (T + 1))
                        v.reduce_sum(musum[:, par:par + 1], x_t[:, T % 4, :], axis=AXX)
                        v.wait_ge(s_sq, T + 1)
                        v.reduce_sum(sqsum[:, par:par + 1], sqd[:, sp, :], axis=AXX).then_inc(s_vr, 1)
                        if par % 2 == 1:
                            sl = par - 1
                            if T >= 4:
                                v.wait_ge(s_r, T // 2 - 1)
                            v.drain()
                            v.tensor_scalar(mu_t[:, sl:sl + 2], musum[:, sl:sl + 2], inv_d, None, ALU.mult)
                            v.drain()
                            v.tensor_scalar(m2_t[:, sl:sl + 2], sqsum[:, sl:sl + 2], inv_d, None, ALU.mult)
                            v.tensor_tensor(msq_t[:, sl:sl + 2], mu_t[:, sl:sl + 2], mu_t[:, sl:sl + 2], ALU.mult)
                            v.drain()
                            v.tensor_tensor(var_t[:, sl:sl + 2], m2_t[:, sl:sl + 2], msq_t[:, sl:sl + 2], ALU.subtract)
                            v.drain()
                            v.tensor_scalar(var_t[:, sl:sl + 2], var_t[:, sl:sl + 2], 1e-5, None, ALU.add)
                            v.drain()
                            v.reciprocal(rec_t[:, sl:sl + 2], var_t[:, sl:sl + 2])
                            v.drain().then_inc(s_bc, 1)
                    if w >= 1:
                        for j in range(16):
                            kv_evac(v, (w - 1) * 16 + j)
                    for t in range(TPW):
                        T = w * TPW + t
                        par = T % 4
                        if par % 2 == 0:
                            v.wait_ge(s_r, T // 2 + 1)
                            v.tensor_tensor(bp_t[:, par:par + 2], mu_t[:, par:par + 2],
                                            r_t[:, par:par + 2], ALU.mult)
                            v.drain()
                            v.tensor_scalar(bp_t[:, par:par + 2], bp_t[:, par:par + 2], -1.0, None, ALU.mult)
                            v.drain()
                        if T >= 3:
                            v.wait_ge(s_trp, T - 2)
                        v.tensor_scalar(ln_tm[:, T % 3, :], x_t[:, T % 4, :], r_t[:, par:par + 1],
                                        bp_t[:, par:par + 1], ALU.mult, ALU.add).then_inc(s_ln, 1)
                        if t >= 1:
                            lf_evac(v, T - 1)
                    lf_evac(v, w * TPW + 3)
                for j in range(16):
                    kv_evac(v, (NW - 1) * 16 + j)
                v.drain()
                v.nop().then_inc(s_bar, 1)

            @block.scalar
            def _(sc):
                sc.wait_ge(s_d0a, 1)
                sc.activation(msz[:, :], msf[:, :], AF.Square, bias=cst[:, 2:3], scale=cst[:, 1:2])
                sc.drain()
                sc.activation(msz[:, :], msz[:, :], AF.Exp, scale=cst[:, 3:4])
                sc.drain().then_inc(s_a0, 1)
                for STEP in range(NT + 1):
                    if STEP >= 1 and (STEP - 1) % 2 == 1:
                        T = STEP - 1
                        sl = (T % 4) - 1
                        sc.wait_ge(s_bc, T // 2 + 1)
                        sc.activation(r_t[:, sl:sl + 2], rec_t[:, sl:sl + 2], AF.Sqrt)
                        sc.drain().then_inc(s_r, 1)
                    if STEP < NT:
                        T = STEP
                        sc.wait_ge(s_x, 16 * (T + 1))
                        if T >= 2:
                            sc.wait_ge(s_vr, T - 1)
                        sc.activation(sqd[:, T % 2, :], x_t[:, T % 4, :], AF.Square).then_inc(s_sq, 1)
                sc.drain()
                sc.nop().then_inc(s_bar, 1)

            @block.sync
            def _(sy):
                for T in range(NT):
                    xs = T % 4
                    if T >= 4:
                        sy.wait_ge(s_ln, T - 3)
                    sy.dma_start(out=x_t[:, xs, :], in_=xb_h[T * 128:(T + 1) * 128, :]).then_inc(s_x, 16)
                sy.wait_ge(s_x, 16 * NT)
                sy.nop().then_inc(s_bar, 1)

            @block.tensor
            def _(tn):
                tn.wait_ge(s_d0, 1)
                tn.wait_ge(s_w1, 32)
                for w in range(NW):
                    for t in range(TPW):
                        T = w * TPW + t
                        par = T % 4
                        tn.wait_ge(s_ln, T + 1)
                        if T >= 1:
                            tn.wait_ge(s_tre, T)
                        for dc in range(OC):
                            ins = tn.transpose(p_tr[:, dc, :], ln_tm[:, T % 3, dc * 128:(dc + 1) * 128],
                                               ident_b[:, :])
                        ins.then_inc(s_trp, 1)
                    tn.wait_ge(s_tre, TPW * (w + 1))
                    for oc in range(OC):
                        gk = w * 16 + oc
                        bi = w * 8 + oc
                        tn.wait_ge(s_wk, 16 * (bi + 1))
                        if gk >= 6:
                            tn.wait_ge(s_pe, gk - 5)
                        for dc in range(OC):
                            ins = tn.matmul(pool_kv[gk % 6][:, :], wk_blk[:, bi % 4, dc, :],
                                            lf[:, w % 2, dc, :], start=(dc == 0), stop=(dc == OC - 1))
                        ins.then_inc(s_pp, 1)
                    for tc in range(TPW):
                        gv0 = w * 16 + 8 + tc * 2
                        if gv0 >= 6:
                            tn.wait_ge(s_pe, gv0 - 4)
                        for dc in range(OC):
                            tn.matmul(pool_kv[gv0 % 6][:, :], lf[:, w % 2, dc, tc * 128:(tc + 1) * 128],
                                      wv_s[:, dc, 0:512], start=(dc == 0), stop=False)
                            tn.matmul(pool_kv[(gv0 + 1) % 6][:, :], lf[:, w % 2, dc, tc * 128:(tc + 1) * 128],
                                      wv_s[:, dc, 512:1024], start=(dc == 0), stop=False)
                        tn.matmul(pool_kv[gv0 % 6][:, :], ones_row[0:1, :], bv_row[0:1, 0:512],
                                  start=False, stop=True).then_inc(s_pp, 1)
                        tn.matmul(pool_kv[(gv0 + 1) % 6][:, :], ones_row[0:1, :], bv_row[0:1, 512:1024],
                                  start=False, stop=True).then_inc(s_pp, 1)
                tn.drain()
                tn.nop().then_inc(s_bar, 1)

        with contextlib.ExitStack() as p2:
            def sb2(name, shape, dt=F32):
                return p2.enter_context(nc.sbuf_tensor(name, shape, dt))

            e_s = sb2("e_s", [128, KC, QB], BF16)
            x2 = sb2("x2", [128, NSUB, D])
            q_s = sb2("q_s", [128, OC, QB], BF16)
            lf2 = sb2("lf2", [128, OC, QB], BF16)
            y1 = sb2("y1", [128, NSUB, D])
            y2 = sb2("y2", [128, 2, 256])
            t2 = sb2("t2", [128, 512])
            sqd2 = sb2("sqd2", [128, 2, 512])
            xq_tm = sb2("xq_tm", [128, D], BF16)
            ex_t = sb2("ex_t", [128, 3, QB])
            q8a = sb2("q8a", [128, 512], FP8)
            q8b = sb2("q8b", [128, 512], FP8)
            mB = sb2("mB", [128, 512], BF16)
            mask8 = sb2("mask8", [128, 512], U8)
            wq_blk = sb2("wq_blk", [128, 4, OC, 128], BF16)
            w2q = sb2("w2q", [128, 2, OC, 256], BF16)

            SEMC = {}
            def cg(s):
                return SEMC.get(id(s), 0)
            def ca(s, n):
                SEMC[id(s)] = cg(s) + n
                return SEMC[id(s)]
            for s_, v_ in [(s_pp, NPP1), (s_pe, NPP1), (s_trp, NTR1), (s_tre, NTR1),
                           (s_ln, NT), (s_sq, NT), (s_vr, NT), (s_r, NT // 2),
                           (s_x, 16 * NT)]:
                SEMC[id(s_)] = v_
            EV = {k: [] for k in ("g", "sy", "sc", "v", "tn")}
            def ev(eng, fn, waits=(), inc=None, n=1):
                EV[eng].append((fn, list(waits), inc, n))
                return ca(inc, n) if inc is not None else None

            def E_lnqT(qb, qs):
                def fn(tn, qs=qs):
                    for dc in range(OC):
                        ins = tn.transpose(p_tr[:, dc, :], xq_tm[:, dc * 128:(dc + 1) * 128], ident_b[:, :])
                    return ins
                return fn

            def E_tr_evac(dst, qs):
                def fn(v, dst=dst, qs=qs):
                    return v.tensor_copy(dst[:, :, qs * 128:(qs + 1) * 128], p_tr[:, :, :])
                return fn

            def E_qproj(qb, oc):
                def fn(tn, qb=qb, oc=oc):
                    bi = qb * OC + oc
                    for dc in range(OC):
                        ins = tn.matmul(pool_q[oc % 4][:, 0:QB], wq_blk[:, bi % 4, dc, :],
                                        lf2[:, dc, :], start=(dc == 0), stop=(dc == OC - 1))
                    return ins
                return fn

            def E_qevac(qb, oc):
                def fn(v, oc=oc):
                    return v.tensor_scalar(q_s[:, oc, :], pool_q[oc % 4][:, 0:QB],
                                           bq_t[:, oc:oc + 1], None, ALU.add)
                return fn

            def E_scores(qb, kc):
                def fn(tn, kc=kc):
                    for oc in range(OC):
                        ins = tn.matmul(pool_sc[kc % 6][:, 0:QB], k_s[:, oc, kc * 128:(kc + 1) * 128],
                                        q_s[:, oc, :], start=(oc == 0), stop=(oc == OC - 1))
                    return ins
                return fn

            def E_exp(qb, kc):
                def fn(sc, kc=kc):
                    return sc.activation(ex_t[:, kc % 3, :], pool_sc[kc % 6][:, 0:QB], AF.Exp,
                                         bias=cst[:, 0:1], scale=msc[:, kc:kc + 1])
                return fn

            def E_cast(qb, kc):
                def fn(v, kc=kc):
                    return v.tensor_copy(e_s[:, kc, :], ex_t[:, kc % 3, :])
                return fn

            def E_lrow(qb):
                def fn(tn):
                    for kc in range(KC):
                        ins = tn.matmul(p_l[0:1, :], ones_col[:, :], e_s[:, kc, :],
                                        start=(kc == 0), stop=(kc == KC - 1))
                    return ins
                return fn

            def E_av(qb, dc):
                def fn(tn, dc=dc):
                    for kc in range(KC):
                        ins = tn.matmul(pool_av[dc % 4][:, 0:QB], v_s[:, kc, dc * 128:(dc + 1) * 128],
                                        e_s[:, kc, :], start=(kc == 0), stop=(kc == KC - 1))
                    return ins
                return fn

            def E_avevac(qb, dc):
                def fn(v, dc=dc):
                    return v.tensor_copy(q_s[:, dc, :], pool_av[dc % 4][:, 0:QB])
                return fn

            def E_proj(qb, oq, qs):
                def fn(tn, qb=qb, oq=oq, qs=qs):
                    QI = qb * 8 + oq
                    for dc in range(OC):
                        ins = tn.matmul(pool_pf[(oq * 2 + qs) % 6][:, 0:256],
                                        q_s[:, dc, qs * 128:(qs + 1) * 128],
                                        w2q[:, QI % 2, dc, :], start=(dc == 0), stop=(dc == OC - 1))
                    return ins
                return fn

            def E_projstt(qb, oq, qs):
                def fn(v, oq=oq, qs=qs):
                    return v.scalar_tensor_tensor(y1[:, qs, oq * 256:(oq + 1) * 256],
                                                  pool_pf[(oq * 2 + qs) % 6][:, 0:256], r_l[:, qs:qs + 1],
                                                  y1[:, qs, oq * 256:(oq + 1) * 256], ALU.mult, ALU.add)
                return fn

            def E_xqT(qb, qs):
                def fn(tn):
                    for dc in range(OC):
                        ins = tn.transpose(p_tr[:, dc, :], xq_tm[:, dc * 128:(dc + 1) * 128], ident_b[:, :])
                    return ins
                return fn

            def E_ffn(qb, oq, qs):
                def fn(tn, qb=qb, oq=oq, qs=qs):
                    QI = qb * 8 + 4 + oq
                    for dc in range(OC):
                        tn.matmul(pool_pf[(oq * 2 + qs) % 6][:, 0:256], lf2[:, dc, qs * 128:(qs + 1) * 128],
                                  w2q[:, QI % 2, dc, :], start=(dc == 0), stop=False)
                    return tn.matmul(pool_pf[(oq * 2 + qs) % 6][:, 0:256], ones_row[0:1, :],
                                     fb_div[0:1, oq * 256:(oq + 1) * 256], start=False, stop=True)
                return fn

            def E_ffnstt(qb, oq, qs):
                def fn(v, oq=oq, qs=qs):
                    yg = oq * 2 + qs
                    return v.scalar_tensor_tensor(y2[:, yg % 2, :], pool_pf[(oq * 2 + qs) % 6][:, 0:256],
                                                  stot_bc[:, :1], y1[:, qs, oq * 256:(oq + 1) * 256],
                                                  ALU.mult, ALU.add)
                return fn

            def E_y2dma(qb, oq, qs):
                def fn(sy, qb=qb, oq=oq, qs=qs):
                    r0 = qb * QB + qs * 128
                    return sy.dma_start(out=y_h[r0:r0 + 128, oq * 256:(oq + 1) * 256],
                                        in_=y2[:, (oq * 2 + qs) % 2, :])
                return fn

            def E_sq(src_ap_fn, h):
                def fn(sc, h=h):
                    return sc.activation(sqd2[:, h, :], src_ap_fn()[:, h * 512:(h + 1) * 512], AF.Square)
                return fn

            def E_sqv(src_ap_fn, h):
                def fn(v, h=h):
                    s = src_ap_fn()[:, h * 512:(h + 1) * 512]
                    return v.tensor_tensor(sqd2[:, h, :], s, s, ALU.mult)
                return fn

            def E_red(src_ap_fn, slot):
                def fn(v, slot=slot):
                    v.reduce_sum(musum[:, slot:slot + 1], src_ap_fn(), axis=AXX)
                    v.reduce_sum(sqsum[:, slot:slot + 1], sqd2[:, 0, :], axis=AXX)
                    v.reduce_sum(msq_t[:, slot:slot + 1], sqd2[:, 1, :], axis=AXX)
                    v.drain()
                    v.tensor_scalar(sqsum[:, slot:slot + 1], sqsum[:, slot:slot + 1],
                                    msq_t[:, slot:slot + 1], None, ALU.add)
                    v.drain()
                    v.tensor_scalar(mu_t[:, slot:slot + 1], musum[:, slot:slot + 1], inv_d, None, ALU.mult)
                    v.drain()
                    v.tensor_scalar(m2_t[:, slot:slot + 1], sqsum[:, slot:slot + 1], inv_d, None, ALU.mult)
                    v.drain()
                    v.tensor_scalar(msq_t[:, slot:slot + 1], mu_t[:, slot:slot + 1],
                                    mu_t[:, slot:slot + 1], None, ALU.mult)
                    v.drain()
                    v.tensor_scalar(var_t[:, slot:slot + 1], m2_t[:, slot:slot + 1],
                                    msq_t[:, slot:slot + 1], 1e-5, ALU.subtract, ALU.add)
                    v.drain()
                    v.reciprocal(rec_t[:, slot:slot + 1], var_t[:, slot:slot + 1])
                    return v.drain()
                return fn

            def E_sqrt(slot):
                def fn(sc, slot=slot):
                    sc.activation(r_t[:, slot:slot + 1], rec_t[:, slot:slot + 1], AF.Sqrt)
                    return sc.drain()
                return fn

            def E_bp1(slot):
                def fn(v, slot=slot):
                    v.tensor_scalar(bp_t[:, slot:slot + 1], mu_t[:, slot:slot + 1],
                                    r_t[:, slot:slot + 1], -1.0, ALU.mult, ALU.mult)
                    return v.drain()
                return fn

            def E_norm1(qs):
                def fn(v, qs=qs):
                    return v.tensor_scalar(xq_tm[:, :], x2[:, qs, :], r_t[:, qs:qs + 1],
                                           bp_t[:, qs:qs + 1], ALU.mult, ALU.add)
                return fn

            def E_bp2(slot):
                def fn(v, slot=slot):
                    v.tensor_scalar(rp_t[:, slot:slot + 1], r_t[:, slot:slot + 1],
                                    rsin_bc[:, :1], None, ALU.mult)
                    v.drain()
                    v.tensor_scalar(bp_t[:, slot:slot + 1], mu_t[:, slot:slot + 1],
                                    rp_t[:, slot:slot + 1], -1.0, ALU.mult, ALU.mult)
                    return v.drain()
                return fn

            def E_norm2h(qs, h):
                def fn(v, qs=qs, h=h):
                    v.tensor_scalar(t2[:, :], y1[:, qs, h * 512:(h + 1) * 512], rp_t[:, qs:qs + 1],
                                    bp_t[:, qs:qs + 1], ALU.mult, ALU.add)
                    return v.tensor_scalar(t2[:, :], t2[:, :], FP8_MAX, -FP8_MAX, ALU.min, ALU.max)
                return fn

            def E_quant_h(qs, h, part):
                def fn(v, qs=qs, h=h, part=part):
                    xs = xq_tm[:, h * 512:(h + 1) * 512]
                    if part == 0:
                        v.tensor_copy(q8a[:, :], t2[:, :])
                        v.tensor_scalar(q8b[:, :], t2[:, :], 0.5, None, ALU.mult)
                        return v.tensor_copy(xs, q8a[:, :])
                    else:
                        v.tensor_scalar(mB[:, :], q8b[:, :], 2.0, None, ALU.mult)
                        v.tensor_scalar(mask8[:, :], t2[:, :], float(HALF_SMALL), None, ALU.is_ge)
                        v.copy_predicated(xs, mask8[:, :], mB[:, :])
                        v.tensor_scalar(mask8[:, :], t2[:, :], float(-HALF_SMALL), None, ALU.is_le)
                        return v.copy_predicated(xs, mask8[:, :], mB[:, :])
                return fn

            def E_pred(oq, qs):
                def fn(v, oq=oq, qs=qs):
                    pm = musum if qs == 0 else sqsum
                    ps = rp_t if qs == 0 else bp_t
                    sl = y1[:, qs, oq * 256:(oq + 1) * 256]
                    v.reduce_sum(pm[:, oq:oq + 1], sl, axis=AXX)
                    v.tensor_tensor(t2[:, 0:256], sl, sl, ALU.mult)
                    v.drain()
                    return v.reduce_sum(ps[:, oq:oq + 1], t2[:, 0:256], axis=AXX)
                return fn

            def E_redp(qs):
                def fn(v, qs=qs):
                    pm = musum if qs == 0 else sqsum
                    ps = rp_t if qs == 0 else bp_t
                    v.reduce_sum(var_t[:, qs:qs + 1], pm[:, 0:4], axis=AXX)
                    v.reduce_sum(rec_t[:, qs:qs + 1], ps[:, 0:4], axis=AXX)
                    v.drain()
                    v.tensor_scalar(mu_t[:, qs:qs + 1], var_t[:, qs:qs + 1], inv_d, None, ALU.mult)
                    v.drain()
                    v.tensor_scalar(m2_t[:, qs:qs + 1], rec_t[:, qs:qs + 1], inv_d, None, ALU.mult)
                    v.drain()
                    v.tensor_scalar(msq_t[:, qs:qs + 1], mu_t[:, qs:qs + 1],
                                    mu_t[:, qs:qs + 1], None, ALU.mult)
                    v.drain()
                    v.tensor_scalar(var_t[:, qs:qs + 1], m2_t[:, qs:qs + 1],
                                    msq_t[:, qs:qs + 1], 1e-5, ALU.subtract, ALU.add)
                    v.drain()
                    v.reciprocal(rec_t[:, qs:qs + 1], var_t[:, qs:qs + 1])
                    return v.drain()
                return fn

            def E_x2load(qb, qs):
                def fn(sy, qb=qb, qs=qs):
                    r0 = qb * QB + qs * 128
                    return sy.dma_start(out=x2[:, qs, :], in_=xo_h[r0:r0 + 128, :])
                return fn

            def E_lcopy(qb):
                def fn(sc):
                    sc.copy(l_row[0:1, :], p_l[0:1, :])
                    return sc.drain()
                return fn

            def E_redist(qb, ssub):
                def fn(g, ssub=ssub):
                    return g.dma_start(out=l_col[:, ssub:ssub + 1],
                                       in_=l_row[0:1, ssub * 128:(ssub + 1) * 128],
                                       allow_slow_non_contiguous=True)
                return fn

            def E_recip(qb):
                def fn(v):
                    v.reciprocal(r_l[:, :], l_col[:, :])
                    return v.drain()
                return fn

            def E_y1pre(qb, qs):
                def fn(v, qs=qs):
                    return v.tensor_add(y1[:, qs, :], x2[:, qs, :], bo_bc[:, :])
                return fn

            def E_wq(qb, oc):
                def fn(g, qb=qb, oc=oc):
                    bi = qb * OC + oc
                    return g.dma_start(out=wq_blk[:, bi % 4, :, :],
                                       in_=wqb_h[:, oc, :, :])
                return fn

            def E_w2(qb, j, which):
                def fn(g, qb=qb, j=j, which=which):
                    QI = qb * 8 + (0 if which == "o" else 4) + j
                    srcr = wob_h if which == "o" else wfb_h
                    return g.dma_start(out=w2q[:, QI % 2, :, :], in_=srcr[:, j, :, :])
                return fn

            qevac_done = {}
            lnq_done = {}
            quant_done = {}
            lnqT_done = {}
            xqT_done = {}
            cast_done = {}
            exp_done = {}
            scg = {}
            avp = {}
            ave_done = {}
            ppg = {}
            peg = {}
            wqc = {}
            w2c = {}
            y1_done = {}
            lr_c = {}
            le_c = {}
            sqc = {}
            rc = {}
            vrc = {}
            outc = {}

            def sched_LN1_sq(qb, qs):
                ev("v", E_sqv(lambda qb=qb, qs=qs: x2[:, qs, :], 0),
                   waits=[(s_x, 16 * NT + 16 * (qb * 2 + qs + 1))])
                ev("v", E_sqv(lambda qb=qb, qs=qs: x2[:, qs, :], 1))

            def sched_LN1_red(qb, qs):
                k = ("ln1", qb, qs)
                vrc[k] = ev("v", E_red(lambda qb=qb, qs=qs: x2[:, qs, :], qs), inc=s_vr)
                rc[k] = ev("sc", E_sqrt(qs), waits=[(s_vr, vrc[k])], inc=s_r)

            def sched_LN1_norm(qb, qs):
                k = ("ln1", qb, qs)
                ev("v", E_bp1(qs), waits=[(s_r, rc[k])])
                tw = cg(s_trp) + 1 if qs == 1 else cg(s_trp)
                lnq_done[(qb, qs)] = ev("v", E_norm1(qs), waits=[(s_trp, tw)], inc=s_ln)

            def sched_LN1(qb):
                for qs in range(NSUB):
                    sched_LN1_sq(qb, qs)
                    sched_LN1_red(qb, qs)
                    sched_LN1_norm(qb, qs)

            def sched_quant_block(qb, qs, part):
                k = ("ln2", qb, qs)
                if part == 0:
                    ev("v", E_sqv(lambda qb=qb, qs=qs: y1[:, qs, :], 0))
                    ev("v", E_sqv(lambda qb=qb, qs=qs: y1[:, qs, :], 1))
                elif part == 1:
                    vrc[k] = ev("v", E_red(lambda qb=qb, qs=qs: y1[:, qs, :], qs), inc=s_vr)
                elif part == 2:
                    ev("v", E_bp2(qs), waits=[(s_r, rc[k])])
                    ev("v", E_norm2h(qs, 0))
                elif part == 3:
                    ev("v", E_quant_h(qs, 0, 0), waits=[(s_trp, cg(s_trp))])
                    ev("v", E_quant_h(qs, 0, 1))
                elif part == 4:
                    ev("v", E_norm2h(qs, 1))
                    ev("v", E_quant_h(qs, 1, 0))
                elif part == 5:
                    quant_done[(qb, qs)] = ev("v", E_quant_h(qs, 1, 1), inc=s_ln)

            for e in ("g", "sy", "sc", "v", "tn"):
                EV[e].append((None, [(s_bar, 5)], None, 1))
            for qs in range(NSUB):
                ev("sy", E_x2load(0, qs), inc=s_x, n=16)
            for oc in range(4):
                wqc[(0, oc)] = ev("g", E_wq(0, oc), inc=s_w2, n=16)
            sched_LN1(0)

            def emit_ffn(fqb, in_att, qb):
                for oq in range(OQ):
                    for qs in range(NSUB):
                        fi = oq * 2 + qs
                        w = [(s_tre, cg(s_tre)), (s_w2, w2c[fqb * 8 + 4 + oq])]
                        if fi >= 6:
                            w.append((s_pe, peg[("f", fqb, 0, fi - 6)]))
                        elif in_att and 2 <= fi <= 3:
                            w.append((s_ae, peg[("av", qb, 4 + fi)]))
                        elif in_att and 4 <= fi <= 5:
                            w.append((s_ae, peg[("av", qb, fi)]))
                        elif not in_att:
                            pmap = {0: (3, 0), 1: (3, 1), 2: (1, 0), 3: (1, 1),
                                    4: (2, 0), 5: (2, 1)}
                            w.append((s_pe, peg[("pr", fqb) + pmap[fi]]))
                        ppg[("f", fqb, oq, qs)] = ev("tn", E_ffn(fqb, oq, qs), waits=w, inc=s_pp)
                        if qs == 1 and oq < 2:
                            w2c[fqb * 8 + 4 + oq + 2] = ev("g", E_w2(fqb, oq + 2, "f"),
                                                           waits=[(s_pp, ppg[("f", fqb, oq, 1)])],
                                                           inc=s_w2, n=16)
                        if in_att and qs == 1 and oq >= 2:
                            w2c[qb * 8 + (oq - 2)] = ev("g", E_w2(qb, oq - 2, "o"),
                                                        waits=[(s_pp, ppg[("f", fqb, oq, 1)])],
                                                        inc=s_w2, n=16)
                        wv_ = [(s_pp, ppg[("f", fqb, oq, qs)])]
                        og = outc.get((fqb, oq - 1, qs))
                        if og is not None:
                            wv_.append((s_out, og))
                        peg[("f", fqb, oq, qs)] = ev("v", E_ffnstt(fqb, oq, qs),
                                                     waits=wv_, inc=s_pe)
                        outc[(fqb, oq, qs)] = ev("sy", E_y2dma(fqb, oq, qs),
                                                 waits=[(s_pe, peg[("f", fqb, oq, qs)])],
                                                 inc=s_out, n=16)
                        if in_att and fi in (6, 7):
                            pqs = fi - 6
                            y1_done[(qb, pqs)] = ev("v", E_y1pre(qb, pqs), inc=s_y1)
                            if qb + 1 < NQB:
                                ev("sy", E_x2load(qb + 1, pqs),
                                   waits=[(s_y1, y1_done[(qb, pqs)])], inc=s_x, n=16)
                                if fi == 7:
                                    sched_LN1_sq(qb + 1, 0)
                                    sched_LN1_red(qb + 1, 0)

            for qb in range(NQB + 1):
                att = qb < NQB
                tail = qb >= 1
                if att:
                    for qs in range(NSUB):
                        lnqT_done[(qb, qs)] = ev("tn", E_lnqT(qb, qs),
                                                 waits=[(s_ln, lnq_done[(qb, qs)]), (s_tre, cg(s_tre))],
                                                 inc=s_trp)
                        ev("v", E_tr_evac(lf2, qs), waits=[(s_trp, lnqT_done[(qb, qs)]),
                                                           (s_pp, cg(s_pp))], inc=s_tre)
                    for oc in range(OC):
                        w = [(s_w2, wqc[(qb, oc)]), (s_tre, cg(s_tre))]
                        if oc >= 4:
                            w.append((s_pe, peg[("q", qb, oc - 4)]))
                        elif qb >= 1:
                            qmap = {0: (3, 0), 1: (3, 1), 2: (2, 0), 3: (2, 1)}
                            w.append((s_pe, peg[("pr", qb - 1) + qmap[oc]]))
                        ppg[("q", qb, oc)] = ev("tn", E_qproj(qb, oc), waits=w, inc=s_pp)
                        nb, no = divmod(qb * 8 + oc + 4, 8)
                        if nb < NQB:
                            wqc[(nb, no)] = ev("g", E_wq(nb, no),
                                               waits=[(s_pp, ppg[("q", qb, oc)])], inc=s_w2, n=16)
                        peg[("q", qb, oc)] = ev("v", E_qevac(qb, oc),
                                                waits=[(s_pp, ppg[("q", qb, oc)])], inc=s_pe)
                    qevac_done[qb] = cg(s_pe)
                    qsched = {}
                    if tail:
                        for pos, jj in [(2, 0), (9, 1), (16, 6), (23, 7)]:
                            pos = min(max(1, pos * KC // 32), KC - 1)
                            qsched.setdefault(pos, []).append(jj)
                    for kc in range(KC):
                        w = [(s_pe, qevac_done[qb])]
                        if kc >= 6:
                            w.append((s_see, exp_done[(qb, kc - 6)]))
                        if qb >= 1 and kc < 6:
                            prevkc = kc + 6 * ((KC - 1 - kc) // 6)
                            w.append((s_see, exp_done[(qb - 1, prevkc)]))
                        scg[(qb, kc)] = ev("tn", E_scores(qb, kc), waits=w, inc=s_sp)
                        we = [(s_sp, scg[(qb, kc)])]
                        if kc >= 3:
                            we.append((s_sec, cast_done[(qb, kc - 3)]))
                        elif qb >= 1:
                            prevkc = kc + 3 * ((KC - 1 - kc) // 3)
                            we.append((s_sec, cast_done[(qb - 1, prevkc)]))
                        exp_done[(qb, kc)] = ev("sc", E_exp(qb, kc), waits=we, inc=s_see)
                        wc = [(s_see, exp_done[(qb, kc)])]
                        if qb >= 1 and kc == 0:
                            wc.append((s_ap, avp[(qb - 1, 7)]))
                        cast_done[(qb, kc)] = ev("v", E_cast(qb, kc), waits=wc, inc=s_sec)
                        for jj in qsched.get(kc, []):
                            sched_quant_block(qb - 1, jj // 6, jj % 6)
                    if tail:
                        for qs_ in range(NSUB):
                            k_ = ("ln2", qb - 1, qs_)
                            rc[k_] = ev("sc", E_sqrt(qs_), waits=[(s_vr, vrc[k_])], inc=s_r)
                    ev("tn", E_lrow(qb), waits=[(s_sec, cast_done[(qb, KC - 1)]),
                                                (s_le, le_c.get(qb - 1, 0))], inc=s_lp)
                    le_c[qb] = ev("sc", E_lcopy(qb), waits=[(s_lp, cg(s_lp)),
                                                            (s_lr, lr_c.get(qb - 1, 0))], inc=s_le)
                    for ssub in range(NSUB):
                        lr_c[qb] = ev("g", E_redist(qb, ssub),
                                      waits=([(s_le, le_c[qb])] if ssub == 0 else []) +
                                            ([(s_rc, cg(s_rc))] if ssub == 0 else []),
                                      inc=s_lr, n=16)
                    for dc in range(OC):
                        w = [(s_sec, cast_done[(qb, KC - 1)])]
                        if dc >= 4:
                            w.append((s_ae, peg[("av", qb, dc - 4)]))
                        avp[(qb, dc)] = ev("tn", E_av(qb, dc), waits=w, inc=s_ap)
                        peg[("av", qb, dc)] = ev("v", E_avevac(qb, dc),
                                                 waits=[(s_ap, avp[(qb, dc)])], inc=s_ae)
                        if tail:
                            if dc <= 2:
                                sched_quant_block(qb - 1, 0, dc + 2)
                            elif dc == 3:
                                sched_quant_block(qb - 1, 0, 5)
                                sched_quant_block(qb - 1, 1, 2)
                            elif dc == 4:
                                xqT_done[(qb - 1, 0)] = ev("tn", E_xqT(qb - 1, 0),
                                                           waits=[(s_ln, quant_done[(qb - 1, 0)]),
                                                                  (s_tre, cg(s_tre))], inc=s_trp)
                                ev("v", E_tr_evac(lf2, 0), waits=[(s_trp, xqT_done[(qb - 1, 0)]),
                                                                  (s_pp, cg(s_pp))], inc=s_tre)
                                sched_quant_block(qb - 1, 1, 3)
                            elif dc == 5:
                                sched_quant_block(qb - 1, 1, 4)
                            elif dc == 6:
                                sched_quant_block(qb - 1, 1, 5)
                        if dc == 6:
                            ev("v", E_recip(qb), waits=[(s_lr, lr_c[qb])], inc=s_rc)
                    ave_done[qb] = cg(s_ae)
                    if tail:
                        xqT_done[(qb - 1, 1)] = ev("tn", E_xqT(qb - 1, 1),
                                                   waits=[(s_ln, quant_done[(qb - 1, 1)]),
                                                          (s_tre, cg(s_tre))], inc=s_trp)
                        ev("v", E_tr_evac(lf2, 1), waits=[(s_trp, xqT_done[(qb - 1, 1)]),
                                                          (s_pp, cg(s_pp))], inc=s_tre)
                        emit_ffn(qb - 1, True, qb)
                    else:
                        for qs in range(NSUB):
                            y1_done[(qb, qs)] = ev("v", E_y1pre(qb, qs), inc=s_y1)
                            if qb + 1 < NQB:
                                ev("sy", E_x2load(qb + 1, qs), waits=[(s_y1, y1_done[(qb, qs)])],
                                   inc=s_x, n=16)
                        if qb + 1 < NQB:
                            sched_LN1_sq(qb + 1, 0)
                            sched_LN1_red(qb + 1, 0)
                        for j in range(2):
                            w2c[qb * 8 + j] = ev("g", E_w2(qb, j, "o"), inc=s_w2, n=16)
                    for oq in range(OQ):
                        for qs in range(NSUB):
                            w = [(s_ae, ave_done[qb]), (s_w2, w2c[qb * 8 + oq])]
                            pi = oq * 2 + qs
                            if pi >= 6:
                                w.append((s_pe, peg[("pr", qb, 0, pi - 6)]))
                            elif qb >= 1:
                                fmap = {0: (3, 0), 1: (3, 1), 2: (1, 0), 3: (1, 1),
                                        4: (2, 0), 5: (2, 1)}
                                w.append((s_pe, peg[("f", qb - 1) + fmap[pi]]))
                            ppg[("pr", qb, oq, qs)] = ev("tn", E_proj(qb, oq, qs), waits=w, inc=s_pp)
                            peg[("pr", qb, oq, qs)] = ev("v", E_projstt(qb, oq, qs),
                                                         waits=[(s_pp, ppg[("pr", qb, oq, qs)])], inc=s_pe)
                            if qb + 1 < NQB:
                                if (oq, qs) == (0, 1):
                                    sched_LN1_sq(qb + 1, 1)
                                elif (oq, qs) == (1, 1):
                                    sched_LN1_red(qb + 1, 1)
                                elif (oq, qs) == (3, 0):
                                    sched_LN1_norm(qb + 1, 0)
                            else:
                                ev("v", E_pred(oq, qs))
                        if oq < 2:
                            w2c[qb * 8 + oq + 2] = ev("g", E_w2(qb, oq + 2, "o"),
                                                      waits=[(s_pp, ppg[("pr", qb, oq, 1)])],
                                                      inc=s_w2, n=16)
                        else:
                            w2c[qb * 8 + 4 + (oq - 2)] = ev("g", E_w2(qb, oq - 2, "f"),
                                                            waits=[(s_pp, ppg[("pr", qb, oq, 1)])],
                                                            inc=s_w2, n=16)
                    if qb + 1 < NQB:
                        sched_LN1_norm(qb + 1, 1)
                else:
                    k0 = ("ln2", qb - 1, 0)
                    k1 = ("ln2", qb - 1, 1)
                    vrc[k0] = ev("v", E_redp(0), inc=s_vr)
                    vrc[k1] = ev("v", E_redp(1), inc=s_vr)
                    rc[k0] = ev("sc", E_sqrt(0), waits=[(s_vr, vrc[k0])], inc=s_r)
                    rc[k1] = ev("sc", E_sqrt(1), waits=[(s_vr, vrc[k1])], inc=s_r)
                    for jj in range(2, 6):
                        sched_quant_block(qb - 1, 0, jj)
                    xqT_done[(qb - 1, 0)] = ev("tn", E_xqT(qb - 1, 0),
                                               waits=[(s_ln, quant_done[(qb - 1, 0)]),
                                                      (s_tre, cg(s_tre))], inc=s_trp)
                    ev("v", E_tr_evac(lf2, 0), waits=[(s_trp, xqT_done[(qb - 1, 0)]),
                                                      (s_pp, cg(s_pp))], inc=s_tre)
                    for jj in range(2, 6):
                        sched_quant_block(qb - 1, 1, jj)
                    xqT_done[(qb - 1, 1)] = ev("tn", E_xqT(qb - 1, 1),
                                               waits=[(s_ln, quant_done[(qb - 1, 1)]),
                                                      (s_tre, cg(s_tre))], inc=s_trp)
                    ev("v", E_tr_evac(lf2, 1), waits=[(s_trp, xqT_done[(qb - 1, 1)]),
                                                      (s_pp, cg(s_pp))], inc=s_tre)
                    emit_ffn(qb - 1, False, qb)
            EV["sy"].append((None, [(s_out, 16 * 8 * NQB)], None, 1))

            def run_events(e, key):
                for fn, waits, inc, n in EV[key]:
                    for s_, val in waits:
                        if val > 0:
                            e.wait_ge(s_, val)
                    if fn is not None:
                        ins = fn(e)
                        if inc is not None:
                            ins.then_inc(inc, n)

            @block.gpsimd
            def _(g):
                run_events(g, "g")

            @block.sync
            def _(sy):
                run_events(sy, "sy")

            @block.scalar
            def _(sc):
                run_events(sc, "sc")

            @block.vector
            def _(v):
                run_events(v, "v")

            @block.tensor
            def _(tn):
                run_events(tn, "tn")

    return nc


def kernel(L0_attn_x, L0_attn_Wq, L0_attn_bq, L0_attn_Wk, L0_attn_bk,
           L0_attn_Wv, L0_attn_bv, L0_attn_Wo, L0_attn_bo,
           L0_ffn_weight_fp8, L0_ffn_bias, L0_ffn_input_scale, L0_ffn_weight_scale):
    x = np.ascontiguousarray(np.asarray(L0_attn_x, dtype=np.float32))
    B, S, D = x.shape
    SQ = S // 2
    OC = D // 128
    OQ = D // 256

    def blk4(w, cols):
        nb = D // cols
        return np.ascontiguousarray(
            np.asarray(w, np.float32).T.reshape(OC, 128, nb, cols).transpose(1, 2, 0, 3))

    wqt = blk4(L0_attn_Wq, 128)
    wkt = blk4(L0_attn_Wk, 128)
    wvt = np.ascontiguousarray(
        np.asarray(L0_attn_Wv, np.float32).T.reshape(OC, 128, D).transpose(1, 0, 2))
    wot = blk4(L0_attn_Wo, 256)
    wft = blk4(L0_ffn_weight_fp8, 256)
    bq = np.ascontiguousarray(np.asarray(L0_attn_bq, np.float32))
    bk = np.ascontiguousarray(np.asarray(L0_attn_bk, np.float32))
    bv = np.ascontiguousarray(np.asarray(L0_attn_bv, np.float32))
    bo = np.ascontiguousarray(np.asarray(L0_attn_bo, np.float32))
    fb = np.ascontiguousarray(np.asarray(L0_ffn_bias, np.float32))
    sin = np.ascontiguousarray(np.asarray(L0_ffn_input_scale, np.float32))
    sw = np.ascontiguousarray(np.asarray(L0_ffn_weight_scale, np.float32))

    nc = build_nc(S=S, SQ=SQ, D=D, QB=256)
    in_maps = []
    for c in range(8):
        b, h = c // 2, c % 2
        in_maps.append({
            "xb": np.ascontiguousarray(x[b]),
            "xo": np.ascontiguousarray(x[b, h * SQ:(h + 1) * SQ]),
            "wqt": wqt, "wkt": wkt, "wvt": wvt, "wot": wot, "wft": wft,
            "bq": bq, "bk": bk, "bv": bv, "bo": bo, "fb": fb,
            "sin": sin, "sw": sw,
        })
    res = run_bass_kernel_spmd(nc, in_maps, core_ids=list(range(8)))
    out = np.empty((B, S, D), dtype=np.float32)
    for c in range(8):
        b, h = c // 2, c % 2
        out[b, h * SQ:(h + 1) * SQ] = res.results[c]["y"]
    return out
